# version 11
# baseline (speedup 1.0000x reference)
import numpy as np
import ml_dtypes

import concourse.bass as bass
import concourse.mybir as mybir
from concourse.bass_utils import run_bass_kernel_spmd

F32 = mybir.dt.float32
BF16 = mybir.dt.bfloat16
OP = mybir.AluOpType
AF = mybir.ActivationFunctionType
AX = mybir.AxisListType.X

N_CORES = 8
B_PER_CORE = 4
S, D, DIN = 2048, 1024, 2048
DPROJ, DPAD = 4384, 4608
NE = 16
EPS = 1e-5
ALPHA = 1.702
GROUP, MACRO, CHUNK = 1024, 512, 128
NG, NMAC, NCHK = 2, 2, 8
NMT = 35
NPAIR_IN = 18
DCE = True
NOC1 = 8 if DCE else 16
NPAIR_M1 = NOC1 // 2
BF = ml_dtypes.bfloat16
import os as _os
STAGE = int(_os.environ.get("KSTAGE", "5"))
KSCAN = int(_os.environ.get("KSCAN", "99"))
KMOE = int(_os.environ.get("KMOE", "99"))
KSWG = int(_os.environ.get("KSWG", "99"))

ENGS = ("v", "s", "t", "dx", "dw", "dg")


class Tracker:
    def __init__(self, nc, per_iter):
        self.nc = nc
        self.per_iter = dict(per_iter)
        self.sems = {}
        self.cnt = {e: 0 for e in ENGS}
        self.base = {e: 0 for e in ENGS}
        self.b = None
        self.lw = {}
        self.lr = {}
        self.satisfied = {}
        self.barsem = None
        self.bar_target = 0
        for e in ENGS:
            cm = nc.semaphore("sem_" + e)
            self.sems[e] = cm.__enter__()
        self.esems = {}
        self.qregs = {}
        self.pending = {}
        for e in ("dx", "dw", "dg"):
            cm = nc.semaphore("echo_" + e)
            self.esems[e] = cm.__enter__()
            self.qregs[e] = self.engine(e).alloc_register("qreg_" + e)
            self.engine(e).reg_mov(self.qregs[e], 0)
            self.pending[e] = 0

    def engine(self, e):
        nc = self.nc
        return {"v": nc.vector, "s": nc.scalar, "t": nc.tensor,
                "dx": nc.scalar, "dw": nc.sync, "dg": nc.gpsimd}[e]

    def flush(self, e):
        p = self.pending.get(e, 0)
        if not p:
            return
        q = self.engine(e)
        q.reg_add(self.qregs[e], self.qregs[e], 16 * p)
        q.wait_ge(self.sems[e], self.qregs[e])
        q.sem_inc(self.esems[e], p)
        self.cnt[e] += p
        self.pending[e] = 0
        self.satisfied[(e, e)] = self.cnt[e]

    def _wait(self, waiter, sem_eng, cnt):
        if cnt <= 0:
            return
        if sem_eng in self.esems:
            if cnt > self.cnt[sem_eng]:
                self.flush(sem_eng)
            cnt = self.cnt[sem_eng]
        key = (waiter, sem_eng)
        if self.satisfied.get(key, 0) >= cnt:
            return
        self.satisfied[key] = cnt
        sem = self.esems[sem_eng] if sem_eng in self.esems else self.sems[sem_eng]
        self.engine(waiter).wait_ge(sem, self.base[sem_eng] + cnt)

    def op(self, eng, emit, reads=(), writes=(), n=1):
        need = {}
        for k in reads:
            w = self.lw.get(k)
            if w:
                need[w[0]] = max(need.get(w[0], 0), w[1])
        for k in writes:
            w = self.lw.get(k)
            if w:
                need[w[0]] = max(need.get(w[0], 0), w[1])
            for re_, rc in self.lr.get(k, {}).items():
                need[re_] = max(need.get(re_, 0), rc)
        for se, cn in sorted(need.items()):
            if se == eng:
                if eng in self.esems and cn > self.cnt[eng]:
                    self.flush(eng)
                cn = min(cn, self.cnt[eng])
            self._wait(eng, se, cn)
        instr = emit(self.engine(eng))
        if n > 0:
            if eng in self.esems:
                instr.then_inc(self.sems[eng], 16)
                self.pending[eng] += 1
            else:
                instr.then_inc(self.sems[eng], n)
                self.cnt[eng] += n
        if eng in self.esems:
            c = self.cnt[eng] + self.pending[eng]
        else:
            c = self.cnt[eng] + (1 if n == 0 else 0)
        for k in reads:
            self.lr.setdefault(k, {})[eng] = max(self.lr.get(k, {}).get(eng, 0), c)
        for k in writes:
            self.lw[k] = (eng, c)
            self.lr[k] = {}
        return c

    def join_all(self):
        for e in ("dx", "dw", "dg"):
            self.flush(e)
        for e in ENGS:
            for se in ENGS:
                if se != e:
                    self._wait(e, se, self.cnt[se])
        self.lw.clear()
        self.lr.clear()

    def reset_sems(self, b=None):
        nc = self.nc
        if self.barsem is None:
            cm = nc.semaphore("sem_bar")
            self.barsem = cm.__enter__()
        t1 = self.bar_target + 5
        t2 = self.bar_target + 10
        if b is not None:
            t1 = b * 10 + t1
            t2 = b * 10 + t2
        phys_engs = (nc.vector, nc.scalar, nc.tensor, nc.sync, nc.gpsimd)
        for phys in phys_engs:
            phys.sem_inc(self.barsem, 1)
            phys.wait_ge(self.barsem, t1)
        owners = {"v": nc.vector, "s": nc.scalar, "t": nc.tensor,
                  "dx": nc.scalar, "dw": nc.sync, "dg": nc.gpsimd}
        for e in ENGS:
            if e in self.esems:
                owners[e].sem_clear(self.esems[e])
            else:
                owners[e].sem_clear(self.sems[e])
        for phys in phys_engs:
            phys.sem_inc(self.barsem, 1)
            phys.wait_ge(self.barsem, t2)
        self.bar_target += 10
        for e in ENGS:
            self.base[e] = 0
            self.cnt[e] = 0
        self.satisfied.clear()
        self.lw.clear()
        self.lr.clear()


def host_layouts(inp):
    perm = np.concatenate([np.arange(0, 2048, 2), np.arange(1, 2048, 2)])
    h = {}
    w_inT = np.zeros((D, DPAD), np.float32)
    w_inT[:, :DPROJ] = np.asarray(inp["L0_attn_in_proj_w"], np.float32).T
    h["w_inT"] = w_inT
    h["w1T"] = np.ascontiguousarray(
        np.asarray(inp["L0_ffn_mlp1_w"], np.float32)[:, perm, :].transpose(0, 2, 1))
    h["w2T"] = np.ascontiguousarray(np.asarray(inp["L0_ffn_mlp2_w"], np.float32).transpose(0, 2, 1))
    h["woT"] = np.ascontiguousarray(np.asarray(inp["L0_attn_out_proj_w"], np.float32).T)
    h["gwT"] = np.ascontiguousarray(np.asarray(inp["L0_ffn_gate_w"], np.float32).T)
    h["norm_w_c"] = np.ascontiguousarray(np.asarray(inp["L0_attn_norm_w"], np.float32).reshape(8, 128).T)
    h["ffn_scale_c"] = np.ascontiguousarray(np.asarray(inp["L0_ffn_scale"], np.float32).reshape(8, 128).T)
    h["gate_w_c"] = np.ascontiguousarray(np.asarray(inp["L0_attn_gate_w"], np.float32).reshape(16, 128).T)
    h["conv_w_c"] = np.ascontiguousarray(
        np.asarray(inp["L0_attn_conv_w"], np.float32).reshape(18, 128, 4).transpose(1, 0, 2).reshape(128, 72))
    h["conv_b_c"] = np.ascontiguousarray(np.asarray(inp["L0_attn_conv_b"], np.float32).reshape(18, 128).T)
    h["dt_bias_c"] = np.asarray(inp["L0_attn_dt_bias"], np.float32).reshape(32, 1)
    h["A_log_c"] = np.asarray(inp["L0_attn_A_log"], np.float32).reshape(32, 1)
    h["D_row"] = np.asarray(inp["L0_attn_D_param"], np.float32).reshape(1, 32)
    b1p = np.asarray(inp["L0_ffn_mlp1_b"], np.float32)[:, perm][:, :1024]
    h["b1glu_c"] = np.ascontiguousarray(b1p.reshape(16, 8, 128).transpose(2, 0, 1).reshape(128, 128))
    h["b2"] = np.ascontiguousarray(np.asarray(inp["L0_ffn_mlp2_b"], np.float32))
    h["gate_b_row"] = np.asarray(inp["L0_ffn_gate_b"], np.float32).reshape(1, 16)
    h["ident"] = np.eye(128, dtype=BF)
    tri_l = np.zeros((128, 128), np.float32)
    tri_r = np.zeros((128, 128), np.float32)
    for k in range(128):
        tri_l[k, k:] = -1e30
        tri_r[k, :k] = 1.0
    h["tri_l"] = tri_l.astype(BF)
    h["tri_r4"] = np.ascontiguousarray(np.tile(tri_r.astype(BF), (1, 4)))
    h["ones2"] = np.ones((2, 128), BF)
    h["ones1r"] = np.ones((1, 128), BF)
    h["ones128"] = np.ones((128, 1), BF)
    return h


INPUT_SPECS = [
    ("w_inT", [D, DPAD], F32), ("w1T", [NE, D, DIN], F32), ("w2T", [NE, D, D], F32),
    ("woT", [DIN, D], F32), ("gwT", [D, NE], F32),
    ("norm_w_c", [128, 8], F32), ("ffn_scale_c", [128, 8], F32), ("gate_w_c", [128, 16], F32),
    ("conv_w_c", [128, 72], F32), ("conv_b_c", [128, 18], F32),
    ("dt_bias_c", [32, 1], F32), ("A_log_c", [32, 1], F32), ("D_row", [1, 32], F32),
    ("b1glu_c", [128, 128], F32), ("b2", [NE, D], F32), ("gate_b_row", [1, 16], F32),
    ("ident", [128, 128], BF16), ("tri_l", [128, 128], BF16), ("tri_r4", [128, 512], BF16),
    ("ones2", [2, 128], BF16), ("ones1r", [1, 128], BF16), ("ones128", [128, 1], BF16),
]

V, Sc, T = "v", "s", "t"


def build():
    nc = bass.Bass("TRN2", target_bir_lowering=False)
    tr = Tracker(nc, {})
    x = nc.dram_tensor("x", [B_PER_CORE, S, D], F32, kind="ExternalInput").ap()
    out = nc.dram_tensor("out", [B_PER_CORE, S, D], F32, kind="ExternalOutput").ap()
    ins = {}
    for name, shape, dt in INPUT_SPECS:
        ins[name] = nc.dram_tensor(name, shape, dt, kind="ExternalInput").ap()

    wsc_in = nc.dram_tensor("wsc_in", [NPAIR_IN, 128, 2048], BF16, kind="Internal").ap()
    wsc_m1 = nc.dram_tensor("wsc_m1", [NE, NPAIR_M1, 128, 2048], BF16, kind="Internal").ap()
    wsc_m2 = nc.dram_tensor("wsc_m2", [NE, 8, 128, D], BF16, kind="Internal").ap()
    wsc_wo = nc.dram_tensor("wsc_wo", [16, 128, D], BF16, kind="Internal").ap()
    wsc_b2 = nc.dram_tensor("wsc_b2", [NE, 1024], BF16, kind="Internal").ap()
    sc_hilo = nc.dram_tensor("sc_hilo", [64, S], BF16, kind="Internal").ap()

    sbufs = []

    def SB(name, shape, dtype):
        cm = nc.sbuf_tensor(name, shape, dtype)
        t_ = cm.__enter__()
        sbufs.append(cm)
        return t_

    c_ident = SB("c_ident", [128, 128], BF16)
    c_tril = SB("c_tril", [128, 128], BF16)
    c_trir4 = SB("c_trir4", [128, 512], BF16)
    c_ones2 = SB("c_ones2", [2, 128], BF16)
    c_ones1r = SB("c_ones1r", [1, 128], BF16)
    c_ones128 = SB("c_ones128", [128, 1], BF16)
    p_normw = SB("p_normw", [128, 8], F32)
    p_scale = SB("p_scale", [128, 8], F32)
    p_gatewc = SB("p_gatewc", [128, 16], F32)
    p_convw = SB("p_convw", [128, 72], F32)
    p_convb = SB("p_convb", [128, 18], F32)
    p_dtb = SB("p_dtb", [32, 1], F32)
    p_A = SB("p_A", [32, 1], F32)
    p_b1 = SB("p_b1", [128, 128], F32)
    p_b1s = SB("p_b1s", [128, 128], F32)
    p_gbh = SB("p_gbh", [1, 16], BF16)
    p_gbl = SB("p_gbl", [1, 16], BF16)
    gw_hi = SB("gw_hi", [128, 8, 16], BF16)
    gw_lo = SB("gw_lo", [128, 8, 16], BF16)
    d_bc = SB("d_bc", [128, 32], BF16)
    eps_c = SB("eps_c", [128, 1], F32)

    with (nc.sbuf_tensor("plf", [128, 4, DPAD], F32) as plf2,
          nc.sbuf_tensor("plb", [128, 4, DPAD], BF16) as plb2,
          nc.psum_tensor("plps", [128, 64], F32) as plps):
        plslot = [0]

        def plnext():
            plslot[0] ^= 1
            return plf2[:, plslot[0], :], plb2[:, plslot[0], :], "plf%d" % plslot[0], "plb%d" % plslot[0]
        plf, plb = plf2[:, 0, :], plb2[:, 0, :]
        plfk, plbk = "plf0", "plb0"
        for nm, dst in [("ident", c_ident), ("tri_l", c_tril), ("tri_r4", c_trir4),
                        ("ones2", c_ones2), ("ones1r", c_ones1r), ("ones128", c_ones128),
                        ("norm_w_c", p_normw), ("ffn_scale_c", p_scale), ("gate_w_c", p_gatewc),
                        ("conv_w_c", p_convw), ("conv_b_c", p_convb),
                        ("dt_bias_c", p_dtb), ("b1glu_c", p_b1)]:
            tr.op("dw", lambda q, nm=nm, dst=dst: q.dma_start(out=dst[:], in_=ins[nm][:]),
                  writes=("c_" + nm,), n=16)

        items = []
        for k in range(8):
            wl = [(wsc_in[mp], (2 * k) * 128) for mp in range(NPAIR_IN)]
            items.append((ins["w_inT"][k * 128:(k + 1) * 128, :], DPAD, p_normw[:, k:k + 1], "c_norm_w_c",
                          [lambda q, lb, d=d_, c0=c0_: q.dma_start(out=d[:, c0:c0 + 256], in_=lb)
                           for (d_, c0_) in wl]))
        for e in range(NE):
            for k in range(8):
                wl = [(wsc_m1[e, mp], (2 * k) * 128) for mp in range(NPAIR_M1)]
                items.append((ins["w1T"][e, k * 128:(k + 1) * 128, 0:NOC1 * 128], NOC1 * 128,
                              p_scale[:, k:k + 1], "c_ffn_scale_c",
                              [lambda q, lb, d=d_, c0=c0_: q.dma_start(out=d[:, c0:c0 + 256], in_=lb)
                               for (d_, c0_) in wl]))
        for e in range(NE):
            for k in range(8):
                items.append((ins["w2T"][e, k * 128:(k + 1) * 128, :], D, None, None,
                              [lambda q, lb, d=wsc_m2[e, k]: q.dma_start(out=d, in_=lb)]))
        for k in range(16):
            items.append((ins["woT"][k * 128:(k + 1) * 128, :], D, p_gatewc[:, k:k + 1], "c_gate_w_c",
                          [lambda q, lb, d=wsc_wo[k]: q.dma_start(out=d, in_=lb)]))

        def emit_load(i):
            src_ap, ncols, fold_col, fold_key, _ = items[i]
            sl = i % 4
            lf = plf2[:, sl, :]
            tr.op("dw", lambda q, lf=lf, ncols=ncols, src_ap=src_ap: q.dma_start(
                out=lf[:, 0:ncols], in_=src_ap), writes=("plf%d" % sl,), n=16)

        def emit_cast_write(i):
            src_ap, ncols, fold_col, fold_key, wfns = items[i]
            sl = i % 4
            lf = plf2[:, sl, :]
            lb = plb2[:, sl, :]
            if fold_col is not None:
                tr.op(V, lambda e, lf=lf, lb=lb, ncols=ncols, fc=fold_col: e.tensor_scalar(
                    lb[:, 0:ncols], lf[:, 0:ncols], fc, None, OP.mult),
                    reads=("plf%d" % sl, fold_key), writes=("plb%d" % sl,))
            else:
                tr.op(V, lambda e, lf=lf, lb=lb, ncols=ncols: e.tensor_copy(lb[:, 0:ncols], lf[:, 0:ncols]),
                      reads=("plf%d" % sl,), writes=("plb%d" % sl,))
            multi = len(wfns) > 1
            for j, wf in enumerate(wfns):
                seg = lb[:, j * 256:(j + 1) * 256] if multi else lb[:, 0:ncols]
                tr.op("dw", lambda q, wf=wf, seg=seg: wf(q, seg), reads=("plb%d" % sl,), n=16)

        emit_load(0)
        emit_load(1)
        emit_load(2)
        for i in range(len(items)):
            if i + 3 < len(items):
                emit_load(i + 3)
            emit_cast_write(i)

        for k in range(8):
            tr.op("dw", lambda q, k=k: q.dma_start(out=plf[:, 0:16], in_=ins["gwT"][k * 128:(k + 1) * 128, :]),
                  writes=("plf",), n=16)
            tr.op(V, lambda e, k=k: e.tensor_scalar(plf[:, 16:32], plf[:, 0:16], p_scale[:, k:k + 1],
                                                    None, OP.mult), reads=("plf", "c_ffn_scale_c"), writes=("plf",))
            tr.op(V, lambda e, k=k: e.tensor_copy(gw_hi[:, k], plf[:, 16:32]), reads=("plf",), writes=("gw_hi",))
            tr.op(V, lambda e, k=k: e.tensor_copy(plf[:, 32:48], gw_hi[:, k]), reads=("gw_hi",), writes=("plf",))
            tr.op(V, lambda e, k=k: e.tensor_tensor(gw_lo[:, k], plf[:, 16:32], plf[:, 32:48], OP.subtract),
                  reads=("plf",), writes=("gw_lo",))
        for e in range(NE):
            tr.op("dw", lambda q, e=e: q.dma_start(out=plf[0:1, 0:1024], in_=ins["b2"][e:e + 1, :]),
                  writes=("plf",), n=16)
            tr.op(V, lambda en, e=e: en.tensor_scalar(plb[0:1, 0:1024], plf[0:1, 0:1024], 0.125, None, OP.mult),
                  reads=("plf",), writes=("plb",))
            tr.op("dw", lambda q, e=e: q.dma_start(out=wsc_b2[e:e + 1, :], in_=plb[0:1, 0:1024]),
                  reads=("plb",), n=16)
        tr.op("dw", lambda q: q.dma_start(out=plf[0:1, 0:16], in_=ins["gate_b_row"][:]), writes=("plf",), n=16)
        tr.op(V, lambda e: e.tensor_copy(p_gbh[:], plf[0:1, 0:16]), reads=("plf",), writes=("p_gbh",))
        tr.op(V, lambda e: e.tensor_copy(plf[0:1, 16:32], p_gbh[:]), reads=("p_gbh",), writes=("plf",))
        tr.op(V, lambda e: e.tensor_tensor(plf[0:1, 32:48], plf[0:1, 0:16], plf[0:1, 16:32], OP.subtract),
              reads=("plf",), writes=("plf",))
        tr.op(V, lambda e: e.tensor_copy(p_gbl[:], plf[0:1, 32:48]), reads=("plf",), writes=("p_gbl",))
        tr.op("dw", lambda q: q.dma_start(out=plf[0:32, 48:49], in_=ins["A_log_c"][:]), writes=("plf",), n=16)
        tr.op(Sc, lambda e: e.activation(p_A[:], plf[0:32, 48:49], AF.Exp), reads=("plf",), writes=("p_A",))
        tr.op(V, lambda e: e.tensor_scalar(p_A[:], p_A[:], -1.0, None, OP.mult), reads=("p_A",), writes=("p_A",))
        tr.op(V, lambda e: e.tensor_scalar(p_b1s[:], p_b1[:], ALPHA, None, OP.mult),
              reads=("c_b1glu_c",), writes=("p_b1s",))
        tr.op("dw", lambda q: q.dma_start(out=plf[0:1, 0:32], in_=ins["D_row"][:]), writes=("plf",), n=16)
        tr.op(V, lambda e: e.tensor_copy(plb[0:1, 0:32], plf[0:1, 0:32]), reads=("plf",), writes=("plb",))
        tr.op(T, lambda e: e.matmul(plps[:, 0:32], c_ones1r[:], plb[0:1, 0:32], start=True, stop=True),
              reads=("plb", "c_ones1r"), writes=("plps",))
        tr.op(V, lambda e: e.tensor_copy(d_bc[:], plps[:, 0:32]), reads=("plps",), writes=("d_bc",))
        tr.op(V, lambda e: e.memset(eps_c[:], EPS), writes=("eps_c",))
        tr.join_all()
        tr.reset_sems()

    op_b = SB("op_b", [128, NCHK, D], F32)
    rT = SB("rT", [128, 8, GROUP], BF16)
    route8 = SB("route8", [128, NCHK, 16], F32)
    xin = SB("xin", [128, 2, D], F32)
    Bcm = SB("Bcm", [128, GROUP], BF16)
    Ccm = SB("Ccm", [128, GROUP], BF16)
    Hst = SB("Hst", [128, DIN], F32)
    Hbf = SB("Hbf", [128, DIN], BF16)
    acum = SB("acum", [32, MACRO], F32)
    runtp = SB("runtp", [32, 1], F32)
    hilo = SB("hilo", [64, MACRO], BF16)
    zsil = SB("zsil", [128, 16, MACRO], BF16)
    xhcm = SB("xhcm", [128, 16, MACRO], BF16)
    xbco = SB("xbco", [128, 2, MACRO + 3], F32)
    halo = SB("halo", [128, 18, 3], F32)
    rowsA = SB("rowsA", [2, 4096], BF16)
    gtb = SB("gtb", [128, 4096], BF16)
    ptsb = SB("ptsb", [128, 128], BF16)
    xh_tm = SB("xh_tm", [128, DIN], BF16)
    y_tm = SB("y_tm", [128, DIN], BF16)
    y_cm = SB("y_cm", [128, 16, 128], BF16)
    ytmp = SB("ytmp", [128, 512], BF16)
    htmp = SB("htmp", [128, 512], F32)
    ebc = SB("ebc", [128, 32], F32)
    wtk = SB("wtk", [128, 32], BF16)
    lrep = SB("lrep", [128, 32], BF16)
    btm = SB("btm", [128, 128], BF16)
    trtmp = SB("trtmp", [128, 4, 32], BF16)
    bctm = SB("bctm", [128, 32], F32)
    dtm = SB("dtm", [128, MACRO], F32)
    ldtm = SB("ldtm", [32, MACRO], F32)
    runt = SB("runt", [32, 1], F32)
    lntmp = SB("lntmp", [128, D], F32)
    lncols = SB("lncols", [128, 16], F32)
    bn6 = SB("bn6", [128, 12], F32)
    rhib = SB("rhib", [128, D], BF16)
    rlob = SB("rlob", [128, D], BF16)
    rloT = SB("rloT", [128, 8, 128], BF16)
    gsb = SB("gsb", [128, 16], F32)
    t8 = SB("t8", [128, 8], F32)
    gtmp = SB("gtmp", [128, 32], F32)
    chtmp = SB("chtmp", [32, MACRO], F32)
    chbf = SB("chbf", [32, 128], BF16)
    chbf2 = SB("chbf2", [32, 128], BF16)
    sgh = SB("sgh", [1, 128], BF16)
    sgl = SB("sgl", [1, 128], BF16)
    sgrf = SB("sgrf", [1, 128], F32)
    scol = SB("scol", [128, 4], F32)
    wring = SB("wring", [128, 2, 16, 128], BF16)
    woring = SB("woring", [128, 2, 2, D], BF16)
    w2full = SB("w2full", [128, 8, D], BF16)
    scm = SB("scm", [128, 8, MACRO], BF16)
    pst = []
    for i in range(8):
        cm = nc.psum_tensor("ps%d" % i, [128, 512], F32)
        pst.append(cm.__enter__())

    def pb(bank, off=0, width=512):
        return pst[bank][:, off:off + width]

    B6A = pb(6, 0, 128)
    B6B = pb(6, 128, 128)
    B6C = pb(6, 256, 128)
    B6D = pb(6, 384, 128)

    def phase_a(b, g, c0, c1):
        for c in range(c0, c1):
            tok0 = g * GROUP + c * CHUNK
            xs = c % 2
            xv = xin[:, xs, :]
            xk = "xin%d" % xs
            tr.op("dg", lambda q, xv=xv: q.dma_start(out=xv, in_=x[b, tok0:tok0 + CHUNK, :]),
                  writes=(xk,), n=16)
            tr.op(V, lambda e, xv=xv: e.bn_stats(bn6[:, 0:6], xv[:, 0:512]), reads=(xk,), writes=("bn6",))
            tr.op(V, lambda e, xv=xv: e.bn_stats(bn6[:, 6:12], xv[:, 512:1024]), reads=(xk,), writes=("bn6",))
            tr.op(V, lambda e: e.bn_aggr(lncols[:, 0:2], bn6[:]), reads=("bn6",), writes=("lncols",))
            tr.op(Sc, lambda e: e.activation(lncols[:, 2:3], lncols[:, 1:2], AF.Ln, bias=eps_c[:]),
                  reads=("lncols",), writes=("lncols",))
            tr.op(Sc, lambda e: e.activation(lncols[:, 3:4], lncols[:, 2:3], AF.Exp, scale=-0.5),
                  reads=("lncols",), writes=("lncols",))
            tr.op(V, lambda e, xv=xv: e.tensor_scalar(lntmp[:], xv, lncols[:, 0:1], lncols[:, 3:4],
                                               OP.subtract, OP.mult),
                  reads=(xk, "lncols"), writes=("lntmp",))
            tr.op(V, lambda e, c=c, xv=xv: e.scalar_tensor_tensor(op_b[:, c, :], lntmp[:], 2.0, xv,
                                                           OP.mult, OP.add),
                  reads=("lntmp", xk), writes=("op%d" % c,))
            tr.op(Sc, lambda e, xv=xv: e.activation(xv, lntmp[:], AF.Square), reads=("lntmp",), writes=(xk,))
            tr.op(V, lambda e, xv=xv: e.reduce_sum(lncols[:, 4:5], xv, axis=AX), reads=(xk,), writes=("lncols",))
            tr.op(Sc, lambda e: e.activation(lncols[:, 5:6], lncols[:, 4:5], AF.Ln, scale=1.0 / D, bias=eps_c[:]),
                  reads=("lncols",), writes=("lncols",))
            tr.op(Sc, lambda e: e.activation(lncols[:, 6:7], lncols[:, 5:6], AF.Exp, scale=-0.5),
                  reads=("lncols",), writes=("lncols",))
            tr.op(V, lambda e: e.tensor_scalar(lntmp[:], lntmp[:], lncols[:, 6:7], None, OP.mult),
                  reads=("lntmp", "lncols"), writes=("lntmp",))
            tr.op(V, lambda e: e.tensor_copy(rhib[:], lntmp[:]), reads=("lntmp",), writes=("rhib",))
            tr.op(V, lambda e: e.tensor_tensor(rlob[:], lntmp[:], rhib[:], OP.subtract),
                  reads=("lntmp", "rhib"), writes=("rlob",))
            for k in range(8):
                pt_hi = B6A.bitcast(BF16)[:, 0:128]
                pt_lo = B6C.bitcast(BF16)[:, 0:128]
                tr.op(T, lambda e, k=k: e.transpose(pt_hi, rhib[:, k * 128:(k + 1) * 128], c_ident[:]),
                      reads=("rhib", "c_ident"), writes=("pb6",))
                tr.op(V, lambda e, k=k, c=c: e.tensor_copy(rT[:, k, c * 128:(c + 1) * 128], pt_hi),
                      reads=("pb6",), writes=("rT%d_%d" % (k, c),))
                tr.op(T, lambda e, k=k: e.transpose(pt_lo, rlob[:, k * 128:(k + 1) * 128], c_ident[:]),
                      reads=("rlob", "c_ident"), writes=("pb6",))
                tr.op(V, lambda e, k=k: e.tensor_copy(rloT[:, k, :], pt_lo), reads=("pb6",), writes=("rloT",))
            gps = pb(5, 0, 16)
            for k in range(8):
                rhi = rT[:, k, c * 128:(c + 1) * 128]
                tr.op(T, lambda e, k=k, rhi=rhi: e.matmul(gps, rhi, gw_hi[:, k], start=(k == 0), stop=False),
                      reads=("rT%d_%d" % (k, c), "gw_hi"), writes=("pb5",), n=0)
                tr.op(T, lambda e, k=k, rhi=rhi: e.matmul(gps, rhi, gw_lo[:, k], start=False, stop=False),
                      reads=("gw_lo",), writes=("pb5",), n=0)
                tr.op(T, lambda e, k=k: e.matmul(gps, rloT[:, k, :], gw_hi[:, k], start=False, stop=False),
                      reads=("rloT",), writes=("pb5",), n=0)
            tr.op(T, lambda e: e.matmul(gps, c_ones1r[:], p_gbh[:], start=False, stop=False),
                  reads=("p_gbh", "c_ones1r"), writes=("pb5",), n=0)
            tr.op(T, lambda e: e.matmul(gps, c_ones1r[:], p_gbl[:], start=False, stop=True),
                  reads=("p_gbl", "c_ones1r"), writes=("pb5",))
            tr.op(V, lambda e: e.tensor_copy(gsb[:], gps), reads=("pb5",), writes=("gsb",))
            tr.op(V, lambda e: e.max(out=t8[:], in_=gsb[:]), reads=("gsb",), writes=("t8",))
            tr.op(V, lambda e: e.tensor_scalar(gtmp[:, 0:16], gsb[:], t8[:, 3:4], None, OP.is_ge),
                  reads=("gsb", "t8"), writes=("gtmp",))
            tr.op(V, lambda e: e.tensor_scalar(gtmp[:, 16:17], t8[:, 0:1], -1.0, None, OP.mult),
                  reads=("t8",), writes=("gtmp",))
            tr.op(Sc, lambda e: e.activation(gsb[:], gsb[:], AF.Exp, bias=gtmp[:, 16:17]),
                  reads=("gsb", "gtmp"), writes=("gsb",))
            tr.op(V, lambda e: e.tensor_tensor(gsb[:], gsb[:], gtmp[:, 0:16], OP.mult),
                  reads=("gsb", "gtmp"), writes=("gsb",))
            tr.op(V, lambda e: e.reduce_sum(gtmp[:, 17:18], gsb[:], axis=AX), reads=("gsb",), writes=("gtmp",))
            tr.op(V, lambda e: e.reciprocal(gtmp[:, 18:19], gtmp[:, 17:18]), reads=("gtmp",), writes=("gtmp",))
            tr.op(V, lambda e, c=c: e.tensor_scalar(route8[:, c, :], gsb[:], gtmp[:, 18:19], 8.0,
                                                    OP.mult, OP.mult),
                  reads=("gsb", "gtmp"), writes=("route%d" % c,))

    def in_proj_macro(b, g, m):
        tok0 = m * MACRO
        glob0 = g * GROUP + m * MACRO
        for mt in range(NMT):
            if mt % 2 == 0:
                mp = mt // 2
                sl = mp % 2
                tr.op("dw", lambda q, mp=mp, sl=sl: q.dma_start(
                    out=wring[:, sl].rearrange("p j c -> p (j c)"), in_=wsc_in[mp]),
                    writes=("wring%d" % sl,), n=16)
            sl = (mt // 2) % 2
            ps = pb(mt % 4)
            psk = "pb%d" % (mt % 4)
            rt_keys = tuple("rT%d_%d" % (k, cc) for k in range(8) for cc in range(4 * m, 4 * m + 4))
            for k in range(8):
                j = 2 * k + (mt % 2)
                tr.op(T, lambda e, j=j, k=k, ps=ps, sl=sl: e.matmul(
                    ps, wring[:, sl, j, :], rT[:, k, tok0:tok0 + MACRO], start=(k == 0), stop=(k == 7)),
                    reads=("wring%d" % sl,) + rt_keys, writes=(psk,), n=1 if k == 7 else 0)
            if mt < 16:
                tr.op(Sc, lambda e, mt=mt, ps=ps: e.activation(zsil[:, mt, :], ps, AF.Silu),
                      reads=(psk,), writes=("zsil",))
            elif mt < 34:
                o = mt - 16
                slot = xbco[:, mt % 2, :]
                sk = "xbco%d" % (mt % 2)
                tr.op(V, lambda e, ps=ps, slot=slot: e.tensor_copy(slot[:, 3:3 + MACRO], ps),
                      reads=(psk,), writes=(sk,))
                tr.op(V, lambda e, slot=slot, o=o: e.tensor_copy(slot[:, 0:3], halo[:, o, :]),
                      reads=("halo%d" % o,), writes=(sk,))
                acc = lntmp[:, 0:MACRO]
                tr.op(V, lambda e, slot=slot, o=o, acc=acc: e.tensor_scalar(
                    acc, slot[:, 0:MACRO], p_convw[:, o * 4:o * 4 + 1], p_convb[:, o:o + 1],
                    OP.mult, OP.add), reads=(sk, "c_conv_w_c", "c_conv_b_c"), writes=("lntmp",))
                for kk in range(1, 4):
                    tr.op(V, lambda e, slot=slot, o=o, kk=kk, acc=acc: e.scalar_tensor_tensor(
                        acc, slot[:, kk:kk + MACRO], p_convw[:, o * 4 + kk:o * 4 + kk + 1], acc,
                        OP.mult, OP.add), reads=(sk, "lntmp"), writes=("lntmp",))
                tr.op(V, lambda e, slot=slot, o=o: e.tensor_copy(halo[:, o, :], slot[:, MACRO:MACRO + 3]),
                      reads=(sk,), writes=("halo%d" % o,))
                if o < 16:
                    dst, dk = xhcm[:, o, :], "xhcm"
                elif o == 16:
                    dst, dk = Bcm[:, tok0:tok0 + MACRO], "Bcm"
                else:
                    dst, dk = Ccm[:, tok0:tok0 + MACRO], "Ccm"
                tr.op(Sc, lambda e, dst=dst, acc=acc: e.activation(dst, acc, AF.Silu),
                      reads=("lntmp",), writes=(dk,))
            else:
                tr.op(V, lambda e, ps=ps: e.tensor_copy(dtm[0:32, :], ps[0:32, :]),
                      reads=(psk,), writes=("dtm",))
        dtr = dtm[0:32, :]
        ldt = ldtm[:]
        tr.op(V, lambda e: e.tensor_scalar(dtr, dtr, p_dtb[:], None, OP.add),
              reads=("dtm", "c_dt_bias_c"), writes=("dtm",))
        tr.op(Sc, lambda e: e.activation(chtmp[:], dtr, AF.Abs), reads=("dtm",), writes=("chtmp",))
        tr.op(Sc, lambda e: e.activation(chtmp[:], chtmp[:], AF.Exp, scale=-1.0), reads=("chtmp",), writes=("chtmp",))
        tr.op(Sc, lambda e: e.activation(chtmp[:], chtmp[:], AF.Ln, bias=1.0), reads=("chtmp",), writes=("chtmp",))
        tr.op(V, lambda e: e.scalar_tensor_tensor(dtr, dtr, 0.0, chtmp[:], OP.max, OP.add),
              reads=("dtm", "chtmp"), writes=("dtm",))
        tr.op(Sc, lambda e: e.activation(ldt, dtr, AF.Ln), reads=("dtm",), writes=("ldtm",))
        tr.op(V, lambda e: e.tensor_scalar(chtmp[:], dtr, p_A[:], None, OP.mult),
              reads=("dtm", "p_A"), writes=("chtmp",))
        if g == 0 and m == 0:
            tr.op(V, lambda e: e.memset(runt[:], 0.0), writes=("runt",))
        tr.op(V, lambda e: e.tensor_copy(runtp[:], runt[:]), reads=("runt",), writes=("runtp",))
        tr.op(V, lambda e: e.tensor_tensor_scan(acum[:], chtmp[:], chtmp[:], runt[:],
                                                OP.add, OP.bypass),
              reads=("chtmp", "runt"), writes=("acum",))
        tr.op(V, lambda e: e.tensor_copy(runt[:], acum[:, MACRO - 1:MACRO]),
              reads=("acum",), writes=("runt",))
        for lc in range(4):
            lo = lc * CHUNK
            sl = hilo[0:32, lo:lo + CHUNK]
            sll = hilo[32:64, lo:lo + CHUNK]
            if lc == 0:
                if g == 0 and m == 0:
                    tr.op(V, lambda e, lo=lo: e.tensor_copy(chtmp[:, 0:CHUNK], acum[:, lo:lo + CHUNK]),
                          reads=("acum",), writes=("chtmp",))
                else:
                    tr.op(V, lambda e, lo=lo: e.tensor_scalar(chtmp[:, 0:CHUNK], acum[:, lo:lo + CHUNK],
                                                              runtp[:], None, OP.subtract),
                          reads=("acum", "runtp"), writes=("chtmp",))
            else:
                tr.op(V, lambda e, lo=lo: e.tensor_scalar(chtmp[:, 0:CHUNK], acum[:, lo:lo + CHUNK],
                                                          acum[:, lo - 1:lo], None, OP.subtract),
                      reads=("acum",), writes=("chtmp",))
            tr.op(V, lambda e, sl=sl: e.tensor_copy(sl, chtmp[:, 0:CHUNK]), reads=("chtmp",), writes=("hilo",))
            tr.op(V, lambda e, sl=sl: e.tensor_copy(chtmp[:, CHUNK:2 * CHUNK], sl), reads=("hilo",),
                  writes=("chtmp",))
            tr.op(V, lambda e, sll=sll: e.tensor_tensor(sll, chtmp[:, 0:CHUNK], chtmp[:, CHUNK:2 * CHUNK],
                                                        OP.subtract), reads=("chtmp",), writes=("hilo",))
        tr.op("dg", lambda q: q.dma_start(out=sc_hilo[:, glob0:glob0 + MACRO], in_=hilo[:]),
              reads=("hilo",), writes=("schilo",), n=16)

    def scan_chunk(b, g, m, lc):
        glob0 = g * GROUP + m * MACRO
        cg = glob0 + lc * CHUNK
        ci = cg // CHUNK
        t0 = m * MACRO + lc * CHUNK
        rr = sc_hilo.rearrange("r (c t) -> c r t", t=CHUNK)
        tr.op("dg", lambda q: q.dma_start(out=rowsA[0:1, :].rearrange("o (h t) -> o h t", t=CHUNK),
                                          in_=rr[ci, 0:32].unsqueeze(0)),
              reads=("schilo",), writes=("rowsA",), n=16)
        tr.op("dg", lambda q: q.dma_start(out=rowsA[1:2, :].rearrange("o (h t) -> o h t", t=CHUNK),
                                          in_=rr[ci, 32:64].unsqueeze(0)),
              reads=("schilo",), writes=("rowsA",), n=16)
        if KSCAN < 11:
            return
        lo = lc * CHUNK
        tr.op("dx", lambda q: q.dma_start_transpose(out=trtmp[:, 0, :], in_=hilo[0:32, lo:lo + CHUNK]),
              reads=("hilo",), writes=("trtmp",), n=16)
        tr.op("dx", lambda q: q.dma_start_transpose(out=trtmp[:, 1, :], in_=hilo[32:64, lo:lo + CHUNK]),
              reads=("hilo",), writes=("trtmp",), n=16)
        if KSCAN < 12:
            return
        ldt = ldtm[:, lo:lo + CHUNK]
        if lc == 0:
            if cg == 0:
                tr.op(V, lambda e: e.tensor_copy(chtmp[:, 2 * CHUNK:3 * CHUNK], acum[:, lo:lo + CHUNK]),
                      reads=("acum",), writes=("chtmp",))
            else:
                tr.op(V, lambda e, lo=lo: e.tensor_scalar(chtmp[:, 2 * CHUNK:3 * CHUNK], acum[:, lo:lo + CHUNK],
                                                          runtp[:], None, OP.subtract),
                      reads=("acum", "runtp"), writes=("chtmp",))
        else:
            tr.op(V, lambda e, lo=lo: e.tensor_scalar(chtmp[:, 2 * CHUNK:3 * CHUNK], acum[:, lo:lo + CHUNK],
                                               acum[:, lo - 1:lo], None, OP.subtract),
                  reads=("acum",), writes=("chtmp",))
        tr.op(V, lambda e: e.tensor_tensor(chtmp[:, 2 * CHUNK:3 * CHUNK], ldt,
                                           chtmp[:, 2 * CHUNK:3 * CHUNK], OP.subtract),
              reads=("ldtm", "chtmp"), writes=("chtmp",))
        tr.op(V, lambda e: e.tensor_copy(chbf[:], chtmp[:, 2 * CHUNK:3 * CHUNK]),
              reads=("chtmp",), writes=("chbf",))
        tr.op(V, lambda e: e.tensor_copy(chtmp[:, 3 * CHUNK:4 * CHUNK], chbf[:]),
              reads=("chbf",), writes=("chtmp",))
        tr.op(V, lambda e: e.tensor_tensor(chbf2[:], chtmp[:, 2 * CHUNK:3 * CHUNK],
                                           chtmp[:, 3 * CHUNK:4 * CHUNK], OP.subtract),
              reads=("chtmp",), writes=("chbf2",))
        tr.op("dx", lambda q: q.dma_start_transpose(out=trtmp[:, 2, :], in_=chbf[:]),
              reads=("chbf",), writes=("trtmp",), n=16)
        tr.op("dx", lambda q: q.dma_start_transpose(out=trtmp[:, 3, :], in_=chbf2[:]),
              reads=("chbf2",), writes=("trtmp",), n=16)
        if KSCAN < 13:
            return
        tr.op(V, lambda e: e.tensor_tensor(bctm[:], trtmp[:, 2, :], trtmp[:, 3, :], OP.add),
              reads=("trtmp",), writes=("bctm",))
        tr.op(V, lambda e: e.tensor_tensor(ebc[:], trtmp[:, 0, :], trtmp[:, 1, :], OP.add),
              reads=("trtmp",), writes=("ebc",))
        tr.op(Sc, lambda e: e.activation(ebc[:], ebc[:], AF.Exp), reads=("ebc",), writes=("ebc",))
        if KSCAN < 2:
            return
        scps = pb(6, 0, 32)
        tr.op(T, lambda e: e.matmul(scps, c_ones2[:], rowsA[:, 127::128], start=True, stop=True),
              reads=("rowsA", "c_ones2"), writes=("pb6",))
        tr.op(Sc, lambda e: e.activation(lrep[:], scps, AF.Exp), reads=("pb6",), writes=("lrep",))
        tr.op(V, lambda e: e.tensor_tensor(gtmp[:, 0:32], bctm[:], scps, OP.add),
              reads=("bctm", "pb6"), writes=("gtmp",))
        tr.op(Sc, lambda e: e.activation(wtk[:], gtmp[:, 0:32], AF.Exp), reads=("gtmp",), writes=("wtk",))
        if KSCAN < 3:
            return
        tr.op(T, lambda e: e.matmul(B6B, Bcm[:, t0:t0 + CHUNK], Ccm[:, t0:t0 + CHUNK], start=True, stop=True),
              reads=("Bcm", "Ccm"), writes=("pb6",))
        tr.op(V, lambda e: e.tensor_copy(ptsb[:], B6B), reads=("pb6",), writes=("ptsb",))
        for o in range(16):
            ptx = B6C.bitcast(BF16)[:, 0:128]
            tr.op(T, lambda e, o=o: e.transpose(ptx, xhcm[:, o, lc * CHUNK:(lc + 1) * CHUNK], c_ident[:]),
                  reads=("xhcm", "c_ident"), writes=("pb6",))
            tr.op(V, lambda e, o=o: e.tensor_copy(xh_tm[:, o * 128:(o + 1) * 128], ptx),
                  reads=("pb6",), writes=("xh_tm",))
        ptx = B6C.bitcast(BF16)[:, 0:128]
        tr.op(T, lambda e: e.transpose(ptx, Bcm[:, t0:t0 + CHUNK], c_ident[:]),
              reads=("Bcm", "c_ident"), writes=("pb6",))
        tr.op(V, lambda e: e.tensor_copy(btm[:], ptx), reads=("pb6",), writes=("btm",))
        if KSCAN < 4:
            return
        if KSCAN < 5:
            return
        for gi in range(8):
            e1 = pb(2 + gi % 2)
            e1k = "pb%d" % (2 + gi % 2)
            tr.op(T, lambda e, gi=gi, e1=e1: e.matmul(e1, c_ones2[:], rowsA[:, gi * 512:(gi + 1) * 512],
                                                      start=True, stop=False),
                  reads=("rowsA", "c_ones2"), writes=(e1k,), n=0)
            tr.op(T, lambda e, e1=e1: e.matmul(e1, c_tril[:], c_trir4[:], start=False, stop=True),
                  reads=("c_tril", "c_trir4"), writes=(e1k,))
            for hh in range(4):
                h = gi * 4 + hh
                tr.op(Sc, lambda e, h=h, hh=hh, e1=e1: e.activation(
                    gtb[:, h * 128:(h + 1) * 128], e1[:, hh * 128:(hh + 1) * 128], AF.Exp,
                    bias=bctm[:, h:h + 1]), reads=(e1k, "bctm"), writes=("gtb",))
            tr.op(V, lambda e, gi=gi: e.tensor_tensor(
                gtb[:, gi * 512:(gi + 1) * 512].rearrange("p (a t) -> p a t", t=128),
                gtb[:, gi * 512:(gi + 1) * 512].rearrange("p (a t) -> p a t", t=128),
                ptsb[:].unsqueeze(1).broadcast_to((128, 4, 128)), OP.mult),
                reads=("gtb", "ptsb"), writes=("gtb",))
        if KSCAN < 6:
            return
        for yg in range(4):
            pia, pik = pb(4), "pb4"
            pie, pek = pb(5), "pb5"
            tr.op(V, lambda e, yg=yg: e.tensor_tensor(
                ytmp[:].rearrange("p (h q) -> p h q", q=64),
                xh_tm[:, yg * 512:(yg + 1) * 512].rearrange("p (h q) -> p h q", q=64),
                d_bc[:, yg * 8:(yg + 1) * 8].unsqueeze(2).broadcast_to((128, 8, 64)), OP.mult),
                reads=("xh_tm", "d_bc"), writes=("ytmp",))
            tr.op(T, lambda e, yg=yg, pia=pia: e.matmul(pia, c_ident[:], ytmp[:],
                                                        start=True, stop=False),
                  reads=("ytmp", "c_ident"), writes=(pik,), n=0)
            for hh in range(8):
                h = yg * 8 + hh
                tr.op(T, lambda e, h=h, hh=hh, pia=pia: e.matmul(
                    pia[:, hh * 64:(hh + 1) * 64], gtb[:, h * 128:(h + 1) * 128],
                    xh_tm[:, h * 64:(h + 1) * 64], start=False, stop=(hh == 7)),
                    reads=("gtb", "xh_tm"), writes=(pik,), n=1 if hh == 7 else 0)
            tr.op(T, lambda e, yg=yg, pie=pie: e.matmul(
                pie, Ccm[:, t0:t0 + CHUNK], Hbf[:, yg * 512:(yg + 1) * 512], start=True, stop=True),
                reads=("Ccm", "Hbf"), writes=(pek,))
            tr.op(V, lambda e, yg=yg, pie=pie: e.tensor_tensor(
                ytmp[:].rearrange("p (h q) -> p h q", q=64), pie.rearrange("p (h q) -> p h q", q=64),
                ebc[:, yg * 8:(yg + 1) * 8].unsqueeze(2).broadcast_to((128, 8, 64)), OP.mult),
                reads=(pek, "ebc"), writes=("ytmp",))
            tr.op(V, lambda e, yg=yg, pia=pia: e.tensor_tensor(
                y_tm[:, yg * 512:(yg + 1) * 512], pia, ytmp[:], OP.add),
                reads=(pik, "ytmp"), writes=("y_tm",))
            tr.op(V, lambda e, yg=yg: e.tensor_tensor(
                ytmp[:].rearrange("p (h q) -> p h q", q=64),
                xh_tm[:, yg * 512:(yg + 1) * 512].rearrange("p (h q) -> p h q", q=64),
                wtk[:, yg * 8:(yg + 1) * 8].unsqueeze(2).broadcast_to((128, 8, 64)), OP.mult),
                reads=("xh_tm", "wtk", "y_tm"), writes=("ytmp",))
            tr.op(T, lambda e, yg=yg, pia=pia: e.matmul(pia, btm[:], ytmp[:],
                                                        start=True, stop=True),
                  reads=("btm", "ytmp", "y_tm"), writes=(pik,))
            tr.op(V, lambda e, yg=yg: e.tensor_tensor(
                htmp[:].rearrange("p (h q) -> p h q", q=64),
                Hst[:, yg * 512:(yg + 1) * 512].rearrange("p (h q) -> p h q", q=64),
                lrep[:, yg * 8:(yg + 1) * 8].unsqueeze(2).broadcast_to((128, 8, 64)), OP.mult),
                reads=("Hst", "lrep"), writes=("htmp",))
            tr.op(V, lambda e, yg=yg, pia=pia: e.tensor_tensor(
                Hst[:, yg * 512:(yg + 1) * 512], htmp[:], pia, OP.add),
                reads=("htmp", pik), writes=("Hst",))
            tr.op(V, lambda e, yg=yg: e.tensor_copy(Hbf[:, yg * 512:(yg + 1) * 512],
                                                    Hst[:, yg * 512:(yg + 1) * 512]),
                  reads=("Hst", pek), writes=("Hbf",))
        if KSCAN < 7:
            return
        for o in range(16):
            pty = B6C.bitcast(BF16)[:, 0:128]
            tr.op(T, lambda e, o=o: e.transpose(pty, y_tm[:, o * 128:(o + 1) * 128], c_ident[:]),
                  reads=("y_tm", "c_ident"), writes=("pb6",))
            tr.op(V, lambda e, o=o: e.tensor_copy(y_cm[:, o, :], pty), reads=("pb6",), writes=("y_cm",))
        zslice = zsil[:, :, lc * CHUNK:(lc + 1) * CHUNK]
        tr.op(V, lambda e: e.tensor_tensor(zslice, y_cm[:], zslice, OP.mult),
              reads=("y_cm", "zsil"), writes=("zsil",))
        tr.op(Sc, lambda e: e.activation(y_tm[:].rearrange("p (o t) -> p o t", t=128), zslice, AF.Square),
              reads=("zsil",), writes=("y_tm",))
        for o in range(16):
            tr.op(T, lambda e, o=o: e.matmul(B6D[0:1, 0:128], c_ones128[:], y_tm[:, o * 128:(o + 1) * 128],
                                             start=(o == 0), stop=(o == 15)),
                  reads=("y_tm", "c_ones128"), writes=("pb6",), n=1 if o == 15 else 0)
        tr.op(Sc, lambda e: e.activation(sgrf[:], B6D[0:1, 0:128], AF.Ln, scale=1.0 / DIN, bias=eps_c[0:1, :]),
              reads=("pb6",), writes=("sgrf",))
        tr.op(Sc, lambda e: e.activation(sgrf[:], sgrf[:], AF.Exp, scale=-0.5), reads=("sgrf",), writes=("sgrf",))
        tr.op(V, lambda e: e.tensor_copy(sgh[:], sgrf[:]), reads=("sgrf",), writes=("sgh",))
        sgtmp = htmp[0:1, 0:256]
        tr.op(V, lambda e: e.tensor_copy(sgtmp[:, 0:128], sgh[:]), reads=("sgh",), writes=("sgtmp", "htmp"))
        tr.op(V, lambda e: e.tensor_tensor(sgtmp[:, 128:256], sgrf[:], sgtmp[:, 0:128], OP.subtract),
              reads=("sgrf", "sgtmp"), writes=("sgtmp",))
        tr.op(V, lambda e: e.tensor_copy(sgl[:], sgtmp[:, 128:256]), reads=("sgtmp",), writes=("sgl",))
        sc2 = pb(6, 0, 1)
        tr.op(T, lambda e: e.matmul(sc2, sgh[:], c_ones1r[0:1, 0:1], start=True, stop=False),
              reads=("sgh", "c_ones1r"), writes=("pb6",), n=0)
        tr.op(T, lambda e: e.matmul(sc2, sgl[:], c_ones1r[0:1, 0:1], start=False, stop=True),
              reads=("sgl", "c_ones1r"), writes=("pb6",))
        tr.op(V, lambda e, lc=lc: e.tensor_copy(scol[:, lc:lc + 1], sc2), reads=("pb6",), writes=("scol",))

    def outproj_macro(b, g, m):
        allb = tuple("pb%d" % i for i in range(8))
        for kk in range(16):
            if kk % 2 == 0:
                sl = (kk // 2) % 2
                tr.op("dw", lambda q, kk=kk, sl=sl: q.dma_start(
                    out=woring[:, sl], in_=wsc_wo[kk:kk + 2].rearrange("a p c -> p a c")),
                    writes=("wor%d" % sl,), n=16)
            for lc in range(4):
                for half in range(2):
                    ti = 2 * lc + half
                    last = (kk == 15 and half == 1)
                    pair_end = (kk % 2 == 1 and lc == 3 and half == 1)
                    tr.op(T, lambda e, kk=kk, lc=lc, half=half, ti=ti: e.matmul(
                        pst[ti][:],
                        zsil[:, kk, lc * CHUNK:(lc + 1) * CHUNK],
                        woring[:, (kk // 2) % 2, kk % 2, half * 512:(half + 1) * 512],
                        start=(kk == 0), stop=(kk == 15)),
                        reads=("zsil", "wor%d" % ((kk // 2) % 2)),
                        writes=allb if last else ("pb%d" % ti,),
                        n=1 if (last or pair_end) else 0)
        for lc in range(4):
            c = 4 * m + lc
            for half in range(2):
                ti = 2 * lc + half
                tr.op(V, lambda e, lc=lc, c=c, half=half, ti=ti: e.scalar_tensor_tensor(
                    op_b[:, c, half * 512:(half + 1) * 512], pst[ti][:], scol[:, lc:lc + 1],
                    op_b[:, c, half * 512:(half + 1) * 512], OP.mult, OP.add),
                    reads=("pb%d" % ti, "scol", "op%d" % c), writes=("op%d" % c,))

    def moe_group(b, g):
        for e in range(NE):
            for hf in range(2):
                tr.op("dw", lambda q, e=e, hf=hf: q.dma_start(
                    out=w2full[:, hf * 4:(hf + 1) * 4, :],
                    in_=wsc_m2[e, hf * 4:(hf + 1) * 4].rearrange("a p c -> p a c")),
                    writes=("w2full",), n=16)
            b2r = gtb[0:1, 0:2048].rearrange("o (a c) -> o a c", a=2)
            tr.op("dg", lambda q, e=e, b2r=b2r: q.dma_start(out=b2r[:, e % 2, :], in_=wsc_b2[e:e + 1, :]),
                  writes=("b2r%d" % (e % 2), "gtb"), n=16)
            if KMOE < 2:
                continue
            for m in range(NMAC):
                tok0 = m * MACRO
                rt_keys = tuple("rT%d_%d" % (k, cc) for k in range(8) for cc in range(4 * m, 4 * m + 4))
                for mt in range(NOC1):
                    if mt % 2 == 0:
                        mp = mt // 2
                        sl = mp % 2
                        tr.op("dw", lambda q, e=e, mp=mp, sl=sl: q.dma_start(
                            out=wring[:, sl].rearrange("p j c -> p (j c)"), in_=wsc_m1[e, mp]),
                            writes=("wring%d" % sl,), n=16)
                    sl = (mt // 2) % 2
                    ps = pb(mt % 4)
                    psk = "pb%d" % (mt % 4)
                    for k in range(8):
                        j = 2 * k + (mt % 2)
                        tr.op(T, lambda e2, j=j, k=k, ps=ps, tok0=tok0, sl=sl: e2.matmul(
                            ps, wring[:, sl, j, :], rT[:, k, tok0:tok0 + MACRO],
                            start=(k == 0), stop=(k == 7)),
                            reads=("wring%d" % sl,) + rt_keys, writes=(psk,), n=1 if k == 7 else 0)
                    if mt < 8 and KMOE >= 3:
                        bcol = p_b1[:, e * 8 + mt:e * 8 + mt + 1]
                        bcols = p_b1s[:, e * 8 + mt:e * 8 + mt + 1]
                        tr.op(V, lambda en, ps=ps, bcol=bcol: en.tensor_scalar(
                            ytmp[:], ps, bcol, 7.0, OP.add, OP.min),
                            reads=(psk, "c_b1glu_c"), writes=("ytmp",))
                        if KSWG >= 2:
                            sgm = y_cm[:, 0:4, :].rearrange("p a c -> p (a c)")
                            tr.op(Sc, lambda en, ps=ps, bcols=bcols, sgm=sgm: en.activation(
                                sgm, ps, AF.Sigmoid, scale=ALPHA, bias=bcols),
                                reads=(psk, "p_b1s", "ytmp"), writes=("sgm", "y_cm"))
                        if KSWG >= 3:
                            tr.op(V, lambda en, mt=mt, sgm=sgm: en.tensor_tensor(
                                scm[:, mt, :], ytmp[:], sgm, OP.mult),
                                reads=("ytmp", "sgm"), writes=("scm",))
                if KMOE < 4:
                    continue
                for lc in range(4):
                    c = 4 * m + lc
                    t0i = 4 + 2 * (lc % 2)
                    for half in range(2):
                        ti = t0i + half
                        for k in range(8):
                            tr.op(T, lambda e2, k=k, half=half, lc=lc, ti=ti: e2.matmul(
                                pst[ti][:],
                                scm[:, k, lc * CHUNK:(lc + 1) * CHUNK],
                                w2full[:, k, half * 512:(half + 1) * 512],
                                start=(k == 0), stop=False),
                                reads=("scm", "w2full"), writes=("pb%d" % ti,), n=0)
                        tr.op(T, lambda e2, half=half, ti=ti, e=e, b2r=b2r: e2.matmul(
                            pst[ti][:], c_ones1r[:],
                            b2r[0:1, e % 2, half * 512:(half + 1) * 512],
                            start=False, stop=True),
                            reads=("b2r%d" % (e % 2), "c_ones1r"), writes=("pb%d" % ti,))
                    if KMOE < 5:
                        continue
                    for half in range(2):
                        ti = t0i + half
                        tr.op(V, lambda en, lc=lc, c=c, half=half, ti=ti, e=e: en.scalar_tensor_tensor(
                            op_b[:, c, half * 512:(half + 1) * 512], pst[ti][:], route8[:, c, e:e + 1],
                            op_b[:, c, half * 512:(half + 1) * 512], OP.mult, OP.add),
                            reads=("pb%d" % ti, "route%d" % c, "op%d" % c), writes=("op%d" % c,))

    with nc.Fori(0, B_PER_CORE) as b:
        tr.op(V, lambda e: e.memset(Hst[:], 0.0), writes=("Hst",))
        tr.op(V, lambda e: e.memset(Hbf[:], 0.0), writes=("Hbf",))
        for o in range(18):
            tr.op(V, lambda e, o=o: e.memset(halo[:, o, :], 0.0), writes=("halo%d" % o,))
        for g in range(NG):
            phase_a(b, g, 0, 4)
            for m in range(NMAC):
                if STAGE >= 2:
                    in_proj_macro(b, g, m)
                if m == 0:
                    phase_a(b, g, 4, NCHK)
                if STAGE >= 3:
                    for lc in range(4):
                        scan_chunk(b, g, m, lc)
                if STAGE >= 4:
                    outproj_macro(b, g, m)
            if STAGE >= 5:
                moe_group(b, g)
            tr.op("dg", lambda q, g=g: q.dma_start(
                out=out[b, g * GROUP:(g + 1) * GROUP, :].rearrange("(c p) d -> p c d", p=128),
                in_=op_b[:]),
                reads=tuple("op%d" % c for c in range(NCHK)), writes=(), n=16)
        tr.join_all()
        tr.reset_sems(b)
    nc.sync.wait_ge(tr.barsem, 10 + 10 * B_PER_CORE)
    return nc


def kernel(**inputs):
    h = host_layouts(inputs)
    x_full = np.ascontiguousarray(np.asarray(inputs["L0_attn_x"], np.float32))
    nc = build()
    in_maps = []
    for core in range(N_CORES):
        m = {"x": x_full[core * B_PER_CORE:(core + 1) * B_PER_CORE]}
        for name, shape, dt in INPUT_SPECS:
            m[name] = h[name]
        in_maps.append(m)
    res = run_bass_kernel_spmd(nc, in_maps, core_ids=list(range(N_CORES)))
    outs = [res.results[i]["out"] for i in range(N_CORES)]
    return np.concatenate(outs, axis=0)


if __name__ == "__main__":
    nc = build()
    n = sum(1 for _ in nc.all_instructions())
    print("instructions:", n)


# version 12
# speedup vs baseline: 1.0161x; 1.0161x over previous
import numpy as np
import ml_dtypes

import concourse.bass as bass
import concourse.mybir as mybir
from concourse.bass_utils import run_bass_kernel_spmd

F32 = mybir.dt.float32
BF16 = mybir.dt.bfloat16
OP = mybir.AluOpType
AF = mybir.ActivationFunctionType
AX = mybir.AxisListType.X

N_CORES = 8
B_PER_CORE = 4
S, D, DIN = 2048, 1024, 2048
DPROJ, DPAD = 4384, 4608
NE = 16
EPS = 1e-5
ALPHA = 1.702
GROUP, MACRO, CHUNK = 1024, 512, 128
NG, NMAC, NCHK = 2, 2, 8
NMT = 35
NPAIR_IN = 18
DCE = True
NOC1 = 8 if DCE else 16
NPAIR_M1 = NOC1 // 2
BF = ml_dtypes.bfloat16
import os as _os
STAGE = int(_os.environ.get("KSTAGE", "5"))
KSCAN = int(_os.environ.get("KSCAN", "99"))
KMOE = int(_os.environ.get("KMOE", "99"))
KSWG = int(_os.environ.get("KSWG", "99"))

ENGS = ("v", "s", "t", "dx", "dw", "dg")


class Tracker:
    def __init__(self, nc, per_iter):
        self.nc = nc
        self.per_iter = dict(per_iter)
        self.sems = {}
        self.cnt = {e: 0 for e in ENGS}
        self.base = {e: 0 for e in ENGS}
        self.b = None
        self.lw = {}
        self.lr = {}
        self.satisfied = {}
        self.barsem = None
        self.bar_target = 0
        for e in ENGS:
            cm = nc.semaphore("sem_" + e)
            self.sems[e] = cm.__enter__()
        self.esems = {}
        self.qregs = {}
        self.pending = {}
        for e in ("dx", "dw", "dg"):
            cm = nc.semaphore("echo_" + e)
            self.esems[e] = cm.__enter__()
            self.qregs[e] = self.engine(e).alloc_register("qreg_" + e)
            self.engine(e).reg_mov(self.qregs[e], 0)
            self.pending[e] = 0

    def engine(self, e):
        nc = self.nc
        return {"v": nc.vector, "s": nc.scalar, "t": nc.tensor,
                "dx": nc.scalar, "dw": nc.sync, "dg": nc.gpsimd}[e]

    def flush(self, e):
        p = self.pending.get(e, 0)
        if not p:
            return
        q = self.engine(e)
        q.reg_add(self.qregs[e], self.qregs[e], 16 * p)
        q.wait_ge(self.sems[e], self.qregs[e])
        q.sem_inc(self.esems[e], p)
        self.cnt[e] += p
        self.pending[e] = 0
        self.satisfied[(e, e)] = self.cnt[e]

    def _wait(self, waiter, sem_eng, cnt):
        if cnt <= 0:
            return
        if sem_eng in self.esems:
            if cnt > self.cnt[sem_eng]:
                self.flush(sem_eng)
            cnt = self.cnt[sem_eng]
        key = (waiter, sem_eng)
        if self.satisfied.get(key, 0) >= cnt:
            return
        self.satisfied[key] = cnt
        sem = self.esems[sem_eng] if sem_eng in self.esems else self.sems[sem_eng]
        self.engine(waiter).wait_ge(sem, self.base[sem_eng] + cnt)

    def op(self, eng, emit, reads=(), writes=(), n=1):
        need = {}
        for k in reads:
            w = self.lw.get(k)
            if w:
                need[w[0]] = max(need.get(w[0], 0), w[1])
        for k in writes:
            w = self.lw.get(k)
            if w:
                need[w[0]] = max(need.get(w[0], 0), w[1])
            for re_, rc in self.lr.get(k, {}).items():
                need[re_] = max(need.get(re_, 0), rc)
        for se, cn in sorted(need.items()):
            if se == eng:
                if eng in self.esems and cn > self.cnt[eng]:
                    self.flush(eng)
                cn = min(cn, self.cnt[eng])
            self._wait(eng, se, cn)
        instr = emit(self.engine(eng))
        if n > 0:
            if eng in self.esems:
                instr.then_inc(self.sems[eng], 16)
                self.pending[eng] += 1
            else:
                instr.then_inc(self.sems[eng], n)
                self.cnt[eng] += n
        if eng in self.esems:
            c = self.cnt[eng] + self.pending[eng]
        else:
            c = self.cnt[eng] + (1 if n == 0 else 0)
        for k in reads:
            self.lr.setdefault(k, {})[eng] = max(self.lr.get(k, {}).get(eng, 0), c)
        for k in writes:
            self.lw[k] = (eng, c)
            self.lr[k] = {}
        return c

    def join_all(self):
        for e in ("dx", "dw", "dg"):
            self.flush(e)
        for e in ENGS:
            for se in ENGS:
                if se != e:
                    self._wait(e, se, self.cnt[se])
        self.lw.clear()
        self.lr.clear()

    def reset_sems(self, b=None):
        nc = self.nc
        if self.barsem is None:
            cm = nc.semaphore("sem_bar")
            self.barsem = cm.__enter__()
        t1 = self.bar_target + 5
        t2 = self.bar_target + 10
        if b is not None:
            t1 = b * 10 + t1
            t2 = b * 10 + t2
        phys_engs = (nc.vector, nc.scalar, nc.tensor, nc.sync, nc.gpsimd)
        for phys in phys_engs:
            phys.sem_inc(self.barsem, 1)
            phys.wait_ge(self.barsem, t1)
        owners = {"v": nc.vector, "s": nc.scalar, "t": nc.tensor,
                  "dx": nc.scalar, "dw": nc.sync, "dg": nc.gpsimd}
        for e in ENGS:
            if e in self.esems:
                owners[e].sem_clear(self.esems[e])
            else:
                owners[e].sem_clear(self.sems[e])
        for phys in phys_engs:
            phys.sem_inc(self.barsem, 1)
            phys.wait_ge(self.barsem, t2)
        self.bar_target += 10
        for e in ENGS:
            self.base[e] = 0
            self.cnt[e] = 0
        self.satisfied.clear()
        self.lw.clear()
        self.lr.clear()


def host_layouts(inp):
    perm = np.concatenate([np.arange(0, 2048, 2), np.arange(1, 2048, 2)])
    h = {}
    w_inT = np.zeros((D, DPAD), np.float32)
    w_inT[:, :DPROJ] = np.asarray(inp["L0_attn_in_proj_w"], np.float32).T
    h["w_inT"] = w_inT
    h["w1T"] = np.ascontiguousarray(
        np.asarray(inp["L0_ffn_mlp1_w"], np.float32)[:, perm, :].transpose(0, 2, 1))
    h["w2T"] = np.ascontiguousarray(np.asarray(inp["L0_ffn_mlp2_w"], np.float32).transpose(0, 2, 1))
    h["woT"] = np.ascontiguousarray(np.asarray(inp["L0_attn_out_proj_w"], np.float32).T)
    h["gwT"] = np.ascontiguousarray(np.asarray(inp["L0_ffn_gate_w"], np.float32).T)
    h["norm_w_c"] = np.ascontiguousarray(np.asarray(inp["L0_attn_norm_w"], np.float32).reshape(8, 128).T)
    h["ffn_scale_c"] = np.ascontiguousarray(np.asarray(inp["L0_ffn_scale"], np.float32).reshape(8, 128).T)
    h["gate_w_c"] = np.ascontiguousarray(np.asarray(inp["L0_attn_gate_w"], np.float32).reshape(16, 128).T)
    h["conv_w_c"] = np.ascontiguousarray(
        np.asarray(inp["L0_attn_conv_w"], np.float32).reshape(18, 128, 4).transpose(1, 0, 2).reshape(128, 72))
    h["conv_b_c"] = np.ascontiguousarray(np.asarray(inp["L0_attn_conv_b"], np.float32).reshape(18, 128).T)
    h["dt_bias_c"] = np.asarray(inp["L0_attn_dt_bias"], np.float32).reshape(32, 1)
    h["A_log_c"] = np.asarray(inp["L0_attn_A_log"], np.float32).reshape(32, 1)
    h["D_row"] = np.asarray(inp["L0_attn_D_param"], np.float32).reshape(1, 32)
    b1p = np.asarray(inp["L0_ffn_mlp1_b"], np.float32)[:, perm][:, :1024]
    h["b1glu_c"] = np.ascontiguousarray(b1p.reshape(16, 8, 128).transpose(2, 0, 1).reshape(128, 128))
    h["b2"] = np.ascontiguousarray(np.asarray(inp["L0_ffn_mlp2_b"], np.float32))
    h["gate_b_row"] = np.asarray(inp["L0_ffn_gate_b"], np.float32).reshape(1, 16)
    h["ident"] = np.eye(128, dtype=BF)
    tri_l = np.zeros((128, 128), np.float32)
    tri_r = np.zeros((128, 128), np.float32)
    for k in range(128):
        tri_l[k, k:] = -1e30
        tri_r[k, :k] = 1.0
    h["tri_l"] = tri_l.astype(BF)
    h["tri_r4"] = np.ascontiguousarray(np.tile(tri_r.astype(BF), (1, 4)))
    h["ones2"] = np.ones((2, 128), BF)
    h["ones1r"] = np.ones((1, 128), BF)
    h["ones128"] = np.ones((128, 1), BF)
    return h


INPUT_SPECS = [
    ("w_inT", [D, DPAD], F32), ("w1T", [NE, D, DIN], F32), ("w2T", [NE, D, D], F32),
    ("woT", [DIN, D], F32), ("gwT", [D, NE], F32),
    ("norm_w_c", [128, 8], F32), ("ffn_scale_c", [128, 8], F32), ("gate_w_c", [128, 16], F32),
    ("conv_w_c", [128, 72], F32), ("conv_b_c", [128, 18], F32),
    ("dt_bias_c", [32, 1], F32), ("A_log_c", [32, 1], F32), ("D_row", [1, 32], F32),
    ("b1glu_c", [128, 128], F32), ("b2", [NE, D], F32), ("gate_b_row", [1, 16], F32),
    ("ident", [128, 128], BF16), ("tri_l", [128, 128], BF16), ("tri_r4", [128, 512], BF16),
    ("ones2", [2, 128], BF16), ("ones1r", [1, 128], BF16), ("ones128", [128, 1], BF16),
]

V, Sc, T = "v", "s", "t"


def build():
    nc = bass.Bass("TRN2", target_bir_lowering=False)
    tr = Tracker(nc, {})
    x = nc.dram_tensor("x", [B_PER_CORE, S, D], F32, kind="ExternalInput").ap()
    out = nc.dram_tensor("out", [B_PER_CORE, S, D], F32, kind="ExternalOutput").ap()
    ins = {}
    for name, shape, dt in INPUT_SPECS:
        ins[name] = nc.dram_tensor(name, shape, dt, kind="ExternalInput").ap()

    wsc_in = nc.dram_tensor("wsc_in", [NPAIR_IN, 128, 2048], BF16, kind="Internal").ap()
    wsc_m1 = nc.dram_tensor("wsc_m1", [NE, NPAIR_M1, 128, 2048], BF16, kind="Internal").ap()
    wsc_m2 = nc.dram_tensor("wsc_m2", [NE, 8, 128, D], BF16, kind="Internal").ap()
    wsc_wo = nc.dram_tensor("wsc_wo", [16, 128, D], BF16, kind="Internal").ap()
    wsc_b2 = nc.dram_tensor("wsc_b2", [NE, 1024], BF16, kind="Internal").ap()
    sc_hilo = nc.dram_tensor("sc_hilo", [64, S], BF16, kind="Internal").ap()

    sbufs = []

    def SB(name, shape, dtype):
        cm = nc.sbuf_tensor(name, shape, dtype)
        t_ = cm.__enter__()
        sbufs.append(cm)
        return t_

    c_ident = SB("c_ident", [128, 128], BF16)
    c_tril = SB("c_tril", [128, 128], BF16)
    c_trir4 = SB("c_trir4", [128, 512], BF16)
    c_ones2 = SB("c_ones2", [2, 128], BF16)
    c_ones1r = SB("c_ones1r", [1, 128], BF16)
    c_ones128 = SB("c_ones128", [128, 1], BF16)
    p_normw = SB("p_normw", [128, 8], F32)
    p_scale = SB("p_scale", [128, 8], F32)
    p_gatewc = SB("p_gatewc", [128, 16], F32)
    p_convw = SB("p_convw", [128, 72], F32)
    p_convb = SB("p_convb", [128, 18], F32)
    p_dtb = SB("p_dtb", [32, 1], F32)
    p_A = SB("p_A", [32, 1], F32)
    p_b1 = SB("p_b1", [128, 128], F32)
    p_b1s = SB("p_b1s", [128, 128], F32)
    p_gbh = SB("p_gbh", [1, 16], BF16)
    p_gbl = SB("p_gbl", [1, 16], BF16)
    gw_hi = SB("gw_hi", [128, 8, 16], BF16)
    gw_lo = SB("gw_lo", [128, 8, 16], BF16)
    d_bc = SB("d_bc", [128, 32], BF16)
    eps_c = SB("eps_c", [128, 1], F32)

    with (nc.sbuf_tensor("plf", [128, 4, DPAD], F32) as plf2,
          nc.sbuf_tensor("plb", [128, 4, DPAD], BF16) as plb2,
          nc.psum_tensor("plps", [128, 64], F32) as plps):
        plslot = [0]

        def plnext():
            plslot[0] ^= 1
            return plf2[:, plslot[0], :], plb2[:, plslot[0], :], "plf%d" % plslot[0], "plb%d" % plslot[0]
        plf, plb = plf2[:, 0, :], plb2[:, 0, :]
        plfk, plbk = "plf0", "plb0"
        for nm, dst in [("ident", c_ident), ("tri_l", c_tril), ("tri_r4", c_trir4),
                        ("ones2", c_ones2), ("ones1r", c_ones1r), ("ones128", c_ones128),
                        ("norm_w_c", p_normw), ("ffn_scale_c", p_scale), ("gate_w_c", p_gatewc),
                        ("conv_w_c", p_convw), ("conv_b_c", p_convb),
                        ("dt_bias_c", p_dtb), ("b1glu_c", p_b1)]:
            tr.op("dw", lambda q, nm=nm, dst=dst: q.dma_start(out=dst[:], in_=ins[nm][:]),
                  writes=("c_" + nm,), n=16)

        items = []
        for k in range(8):
            wl = [(wsc_in[mp], (2 * k) * 128) for mp in range(NPAIR_IN)]
            items.append((ins["w_inT"][k * 128:(k + 1) * 128, :], DPAD, p_normw[:, k:k + 1], "c_norm_w_c",
                          [lambda q, lb, d=d_, c0=c0_: q.dma_start(out=d[:, c0:c0 + 256], in_=lb)
                           for (d_, c0_) in wl]))
        for e in range(NE):
            for k in range(8):
                wl = [(wsc_m1[e, mp], (2 * k) * 128) for mp in range(NPAIR_M1)]
                items.append((ins["w1T"][e, k * 128:(k + 1) * 128, 0:NOC1 * 128], NOC1 * 128,
                              p_scale[:, k:k + 1], "c_ffn_scale_c",
                              [lambda q, lb, d=d_, c0=c0_: q.dma_start(out=d[:, c0:c0 + 256], in_=lb)
                               for (d_, c0_) in wl]))
        for e in range(NE):
            for k in range(8):
                items.append((ins["w2T"][e, k * 128:(k + 1) * 128, :], D, None, None,
                              [lambda q, lb, d=wsc_m2[e, k]: q.dma_start(out=d, in_=lb)]))
        for k in range(16):
            items.append((ins["woT"][k * 128:(k + 1) * 128, :], D, p_gatewc[:, k:k + 1], "c_gate_w_c",
                          [lambda q, lb, d=wsc_wo[k]: q.dma_start(out=d, in_=lb)]))

        def emit_load(i):
            src_ap, ncols, fold_col, fold_key, _ = items[i]
            sl = i % 4
            lf = plf2[:, sl, :]
            tr.op("dw", lambda q, lf=lf, ncols=ncols, src_ap=src_ap: q.dma_start(
                out=lf[:, 0:ncols], in_=src_ap), writes=("plf%d" % sl,), n=16)

        def emit_cast_write(i):
            src_ap, ncols, fold_col, fold_key, wfns = items[i]
            sl = i % 4
            lf = plf2[:, sl, :]
            lb = plb2[:, sl, :]
            if fold_col is not None:
                tr.op(V, lambda e, lf=lf, lb=lb, ncols=ncols, fc=fold_col: e.tensor_scalar(
                    lb[:, 0:ncols], lf[:, 0:ncols], fc, None, OP.mult),
                    reads=("plf%d" % sl, fold_key), writes=("plb%d" % sl,))
            else:
                tr.op(V, lambda e, lf=lf, lb=lb, ncols=ncols: e.tensor_copy(lb[:, 0:ncols], lf[:, 0:ncols]),
                      reads=("plf%d" % sl,), writes=("plb%d" % sl,))
            multi = len(wfns) > 1
            for j, wf in enumerate(wfns):
                seg = lb[:, j * 256:(j + 1) * 256] if multi else lb[:, 0:ncols]
                tr.op("dw", lambda q, wf=wf, seg=seg: wf(q, seg), reads=("plb%d" % sl,), n=16)

        emit_load(0)
        emit_load(1)
        emit_load(2)
        for i in range(len(items)):
            if i + 3 < len(items):
                emit_load(i + 3)
            emit_cast_write(i)

        for k in range(8):
            tr.op("dw", lambda q, k=k: q.dma_start(out=plf[:, 0:16], in_=ins["gwT"][k * 128:(k + 1) * 128, :]),
                  writes=("plf",), n=16)
            tr.op(V, lambda e, k=k: e.tensor_scalar(plf[:, 16:32], plf[:, 0:16], p_scale[:, k:k + 1],
                                                    None, OP.mult), reads=("plf", "c_ffn_scale_c"), writes=("plf",))
            tr.op(V, lambda e, k=k: e.tensor_copy(gw_hi[:, k], plf[:, 16:32]), reads=("plf",), writes=("gw_hi",))
            tr.op(V, lambda e, k=k: e.tensor_copy(plf[:, 32:48], gw_hi[:, k]), reads=("gw_hi",), writes=("plf",))
            tr.op(V, lambda e, k=k: e.tensor_tensor(gw_lo[:, k], plf[:, 16:32], plf[:, 32:48], OP.subtract),
                  reads=("plf",), writes=("gw_lo",))
        for e in range(NE):
            tr.op("dw", lambda q, e=e: q.dma_start(out=plf[0:1, 0:1024], in_=ins["b2"][e:e + 1, :]),
                  writes=("plf",), n=16)
            tr.op(V, lambda en, e=e: en.tensor_scalar(plb[0:1, 0:1024], plf[0:1, 0:1024], 0.125, None, OP.mult),
                  reads=("plf",), writes=("plb",))
            tr.op("dw", lambda q, e=e: q.dma_start(out=wsc_b2[e:e + 1, :], in_=plb[0:1, 0:1024]),
                  reads=("plb",), n=16)
        tr.op("dw", lambda q: q.dma_start(out=plf[0:1, 0:16], in_=ins["gate_b_row"][:]), writes=("plf",), n=16)
        tr.op(V, lambda e: e.tensor_copy(p_gbh[:], plf[0:1, 0:16]), reads=("plf",), writes=("p_gbh",))
        tr.op(V, lambda e: e.tensor_copy(plf[0:1, 16:32], p_gbh[:]), reads=("p_gbh",), writes=("plf",))
        tr.op(V, lambda e: e.tensor_tensor(plf[0:1, 32:48], plf[0:1, 0:16], plf[0:1, 16:32], OP.subtract),
              reads=("plf",), writes=("plf",))
        tr.op(V, lambda e: e.tensor_copy(p_gbl[:], plf[0:1, 32:48]), reads=("plf",), writes=("p_gbl",))
        tr.op("dw", lambda q: q.dma_start(out=plf[0:32, 48:49], in_=ins["A_log_c"][:]), writes=("plf",), n=16)
        tr.op(Sc, lambda e: e.activation(p_A[:], plf[0:32, 48:49], AF.Exp), reads=("plf",), writes=("p_A",))
        tr.op(V, lambda e: e.tensor_scalar(p_A[:], p_A[:], -1.0, None, OP.mult), reads=("p_A",), writes=("p_A",))
        tr.op(V, lambda e: e.tensor_scalar(p_b1s[:], p_b1[:], ALPHA, None, OP.mult),
              reads=("c_b1glu_c",), writes=("p_b1s",))
        tr.op("dw", lambda q: q.dma_start(out=plf[0:1, 0:32], in_=ins["D_row"][:]), writes=("plf",), n=16)
        tr.op(V, lambda e: e.tensor_copy(plb[0:1, 0:32], plf[0:1, 0:32]), reads=("plf",), writes=("plb",))
        tr.op(T, lambda e: e.matmul(plps[:, 0:32], c_ones1r[:], plb[0:1, 0:32], start=True, stop=True),
              reads=("plb", "c_ones1r"), writes=("plps",))
        tr.op(V, lambda e: e.tensor_copy(d_bc[:], plps[:, 0:32]), reads=("plps",), writes=("d_bc",))
        tr.op(V, lambda e: e.memset(eps_c[:], EPS), writes=("eps_c",))
        tr.join_all()
        tr.reset_sems()

    op_b = SB("op_b", [128, NCHK, D], F32)
    rT = SB("rT", [128, 8, GROUP], BF16)
    route8 = SB("route8", [128, NCHK, 16], F32)
    xin = SB("xin", [128, 2, D], F32)
    Bcm = SB("Bcm", [128, GROUP], BF16)
    Ccm = SB("Ccm", [128, GROUP], BF16)
    Hst = SB("Hst", [128, DIN], F32)
    Hbf = SB("Hbf", [128, DIN], BF16)
    acum = SB("acum", [32, MACRO], F32)
    runtp = SB("runtp", [32, 1], F32)
    hilo = SB("hilo", [64, MACRO], BF16)
    zsil = SB("zsil", [128, 16, MACRO], BF16)
    xhcm = SB("xhcm", [128, 16, MACRO], BF16)
    xbco = SB("xbco", [128, 2, MACRO + 3], F32)
    halo = SB("halo", [128, 18, 3], F32)
    rowsA = SB("rowsA", [2, 4096], BF16)
    gtb = SB("gtb", [128, 4096], BF16)
    ptsb = SB("ptsb", [128, 128], BF16)
    xh_tm = SB("xh_tm", [128, DIN], BF16)
    y_tm = SB("y_tm", [128, DIN], BF16)
    y_cm = SB("y_cm", [128, 16, 128], BF16)
    ytmp = SB("ytmp", [128, 512], BF16)
    htmp = SB("htmp", [128, 512], F32)
    ebc = SB("ebc", [128, 32], F32)
    wtk = SB("wtk", [128, 32], BF16)
    lrep = SB("lrep", [128, 32], BF16)
    btm = SB("btm", [128, 128], BF16)
    trtmp = SB("trtmp", [128, 4, 32], BF16)
    bctm = SB("bctm", [128, 32], F32)
    dtm = SB("dtm", [128, MACRO], F32)
    ldtm = SB("ldtm", [32, MACRO], F32)
    runt = SB("runt", [32, 1], F32)
    lntmp = SB("lntmp", [128, D], F32)
    lncols = SB("lncols", [128, 16], F32)
    bn6 = SB("bn6", [128, 12], F32)
    rhib = SB("rhib", [128, D], BF16)
    rlob = SB("rlob", [128, D], BF16)
    rloT = SB("rloT", [128, 8, 128], BF16)
    gsb = SB("gsb", [128, 16], F32)
    t8 = SB("t8", [128, 8], F32)
    gtmp = SB("gtmp", [128, 32], F32)
    chtmp = SB("chtmp", [32, MACRO], F32)
    chbf = SB("chbf", [32, 128], BF16)
    chbf2 = SB("chbf2", [32, 128], BF16)
    sgh = SB("sgh", [1, 128], BF16)
    sgl = SB("sgl", [1, 128], BF16)
    sgrf = SB("sgrf", [1, 128], F32)
    scol = SB("scol", [128, 4], F32)
    wring = SB("wring", [128, 2, 16, 128], BF16)
    woring = SB("woring", [128, 2, 2, D], BF16)
    w2full = SB("w2full", [128, 8, D], BF16)
    scm = SB("scm", [128, 8, MACRO], BF16)
    pst = []
    for i in range(8):
        cm = nc.psum_tensor("ps%d" % i, [128, 512], F32)
        pst.append(cm.__enter__())

    def pb(bank, off=0, width=512):
        return pst[bank][:, off:off + width]

    B6A = pb(6, 0, 128)
    B6B = pb(6, 128, 128)
    B6C = pb(6, 256, 128)
    B6D = pb(6, 384, 128)

    def phase_a(b, g, c0, c1):
        for c in range(c0, c1):
            tok0 = g * GROUP + c * CHUNK
            xs = c % 2
            xv = xin[:, xs, :]
            xk = "xin%d" % xs
            tr.op("dg", lambda q, xv=xv: q.dma_start(out=xv, in_=x[b, tok0:tok0 + CHUNK, :]),
                  writes=(xk,), n=16)
            tr.op(V, lambda e, xv=xv: e.bn_stats(bn6[:, 0:6], xv[:, 0:512]), reads=(xk,), writes=("bn6",))
            tr.op(V, lambda e, xv=xv: e.bn_stats(bn6[:, 6:12], xv[:, 512:1024]), reads=(xk,), writes=("bn6",))
            tr.op(V, lambda e: e.bn_aggr(lncols[:, 0:2], bn6[:]), reads=("bn6",), writes=("lncols",))
            tr.op(Sc, lambda e: e.activation(lncols[:, 2:3], lncols[:, 1:2], AF.Ln, bias=eps_c[:]),
                  reads=("lncols",), writes=("lncols",))
            tr.op(Sc, lambda e: e.activation(lncols[:, 3:4], lncols[:, 2:3], AF.Exp, scale=-0.5),
                  reads=("lncols",), writes=("lncols",))
            tr.op(V, lambda e, xv=xv: e.tensor_scalar(lntmp[:], xv, lncols[:, 0:1], lncols[:, 3:4],
                                               OP.subtract, OP.mult),
                  reads=(xk, "lncols"), writes=("lntmp",))
            tr.op(V, lambda e, c=c, xv=xv: e.scalar_tensor_tensor(op_b[:, c, :], lntmp[:], 2.0, xv,
                                                           OP.mult, OP.add),
                  reads=("lntmp", xk), writes=("op%d" % c,))
            tr.op(Sc, lambda e, xv=xv: e.activation(xv, lntmp[:], AF.Square), reads=("lntmp",), writes=(xk,))
            tr.op(V, lambda e, xv=xv: e.reduce_sum(lncols[:, 4:5], xv, axis=AX), reads=(xk,), writes=("lncols",))
            tr.op(Sc, lambda e: e.activation(lncols[:, 5:6], lncols[:, 4:5], AF.Ln, scale=1.0 / D, bias=eps_c[:]),
                  reads=("lncols",), writes=("lncols",))
            tr.op(Sc, lambda e: e.activation(lncols[:, 6:7], lncols[:, 5:6], AF.Exp, scale=-0.5),
                  reads=("lncols",), writes=("lncols",))
            tr.op(V, lambda e: e.tensor_scalar(lntmp[:], lntmp[:], lncols[:, 6:7], None, OP.mult),
                  reads=("lntmp", "lncols"), writes=("lntmp",))
            tr.op(V, lambda e: e.tensor_copy(rhib[:], lntmp[:]), reads=("lntmp",), writes=("rhib",))
            tr.op(V, lambda e: e.tensor_tensor(rlob[:], lntmp[:], rhib[:], OP.subtract),
                  reads=("lntmp", "rhib"), writes=("rlob",))
            for k in range(8):
                pt_hi = B6A.bitcast(BF16)[:, 0:128]
                pt_lo = B6C.bitcast(BF16)[:, 0:128]
                tr.op(T, lambda e, k=k: e.transpose(pt_hi, rhib[:, k * 128:(k + 1) * 128], c_ident[:]),
                      reads=("rhib", "c_ident"), writes=("pb6",))
                tr.op(V, lambda e, k=k, c=c: e.tensor_copy(rT[:, k, c * 128:(c + 1) * 128], pt_hi),
                      reads=("pb6",), writes=("rT%d_%d" % (k, c),))
                tr.op(T, lambda e, k=k: e.transpose(pt_lo, rlob[:, k * 128:(k + 1) * 128], c_ident[:]),
                      reads=("rlob", "c_ident"), writes=("pb6",))
                tr.op(V, lambda e, k=k: e.tensor_copy(rloT[:, k, :], pt_lo), reads=("pb6",), writes=("rloT",))
            gps = pb(5, 0, 16)
            for k in range(8):
                rhi = rT[:, k, c * 128:(c + 1) * 128]
                tr.op(T, lambda e, k=k, rhi=rhi: e.matmul(gps, rhi, gw_hi[:, k], start=(k == 0), stop=False),
                      reads=("rT%d_%d" % (k, c), "gw_hi"), writes=("pb5",), n=0)
                tr.op(T, lambda e, k=k, rhi=rhi: e.matmul(gps, rhi, gw_lo[:, k], start=False, stop=False),
                      reads=("gw_lo",), writes=("pb5",), n=0)
                tr.op(T, lambda e, k=k: e.matmul(gps, rloT[:, k, :], gw_hi[:, k], start=False, stop=False),
                      reads=("rloT",), writes=("pb5",), n=0)
            tr.op(T, lambda e: e.matmul(gps, c_ones1r[:], p_gbh[:], start=False, stop=False),
                  reads=("p_gbh", "c_ones1r"), writes=("pb5",), n=0)
            tr.op(T, lambda e: e.matmul(gps, c_ones1r[:], p_gbl[:], start=False, stop=True),
                  reads=("p_gbl", "c_ones1r"), writes=("pb5",))
            tr.op(V, lambda e: e.tensor_copy(gsb[:], gps), reads=("pb5",), writes=("gsb",))
            tr.op(V, lambda e: e.max(out=t8[:], in_=gsb[:]), reads=("gsb",), writes=("t8",))
            tr.op(V, lambda e: e.tensor_scalar(gtmp[:, 0:16], gsb[:], t8[:, 3:4], None, OP.is_ge),
                  reads=("gsb", "t8"), writes=("gtmp",))
            tr.op(V, lambda e: e.tensor_scalar(gtmp[:, 16:17], t8[:, 0:1], -1.0, None, OP.mult),
                  reads=("t8",), writes=("gtmp",))
            tr.op(Sc, lambda e: e.activation(gsb[:], gsb[:], AF.Exp, bias=gtmp[:, 16:17]),
                  reads=("gsb", "gtmp"), writes=("gsb",))
            tr.op(V, lambda e: e.tensor_tensor(gsb[:], gsb[:], gtmp[:, 0:16], OP.mult),
                  reads=("gsb", "gtmp"), writes=("gsb",))
            tr.op(V, lambda e: e.reduce_sum(gtmp[:, 17:18], gsb[:], axis=AX), reads=("gsb",), writes=("gtmp",))
            tr.op(V, lambda e: e.reciprocal(gtmp[:, 18:19], gtmp[:, 17:18]), reads=("gtmp",), writes=("gtmp",))
            tr.op(V, lambda e, c=c: e.tensor_scalar(route8[:, c, :], gsb[:], gtmp[:, 18:19], 8.0,
                                                    OP.mult, OP.mult),
                  reads=("gsb", "gtmp"), writes=("route%d" % c,))

    def in_proj_macro(b, g, m):
        tok0 = m * MACRO
        glob0 = g * GROUP + m * MACRO
        for mt in range(NMT):
            if mt % 2 == 0:
                mp = mt // 2
                sl = mp % 2
                tr.op("dw", lambda q, mp=mp, sl=sl: q.dma_start(
                    out=wring[:, sl].rearrange("p j c -> p (j c)"), in_=wsc_in[mp]),
                    writes=("wring%d" % sl,), n=16)
            sl = (mt // 2) % 2
            ps = pb(mt % 2)
            psk = "pb%d" % (mt % 2)
            rt_keys = tuple("rT%d_%d" % (k, cc) for k in range(8) for cc in range(4 * m, 4 * m + 4))
            for k in range(8):
                j = 2 * k + (mt % 2)
                tr.op(T, lambda e, j=j, k=k, ps=ps, sl=sl: e.matmul(
                    ps, wring[:, sl, j, :], rT[:, k, tok0:tok0 + MACRO], start=(k == 0), stop=(k == 7)),
                    reads=("wring%d" % sl,) + rt_keys, writes=(psk,), n=1 if k == 7 else 0)
            if mt < 16:
                tr.op(Sc, lambda e, mt=mt, ps=ps: e.activation(zsil[:, mt, :], ps, AF.Silu),
                      reads=(psk,), writes=("zsil",))
            elif mt < 34:
                o = mt - 16
                slot = xbco[:, mt % 2, :]
                sk = "xbco%d" % (mt % 2)
                tr.op(V, lambda e, ps=ps, slot=slot: e.tensor_copy(slot[:, 3:3 + MACRO], ps),
                      reads=(psk,), writes=(sk,))
                tr.op(V, lambda e, slot=slot, o=o: e.tensor_copy(slot[:, 0:3], halo[:, o, :]),
                      reads=("halo%d" % o,), writes=(sk,))
                acc = lntmp[:, 0:MACRO]
                tr.op(V, lambda e, slot=slot, o=o, acc=acc: e.tensor_scalar(
                    acc, slot[:, 0:MACRO], p_convw[:, o * 4:o * 4 + 1], p_convb[:, o:o + 1],
                    OP.mult, OP.add), reads=(sk, "c_conv_w_c", "c_conv_b_c"), writes=("lntmp",))
                for kk in range(1, 4):
                    tr.op(V, lambda e, slot=slot, o=o, kk=kk, acc=acc: e.scalar_tensor_tensor(
                        acc, slot[:, kk:kk + MACRO], p_convw[:, o * 4 + kk:o * 4 + kk + 1], acc,
                        OP.mult, OP.add), reads=(sk, "lntmp"), writes=("lntmp",))
                tr.op(V, lambda e, slot=slot, o=o: e.tensor_copy(halo[:, o, :], slot[:, MACRO:MACRO + 3]),
                      reads=(sk,), writes=("halo%d" % o,))
                if o < 16:
                    dst, dk = xhcm[:, o, :], "xhcm"
                elif o == 16:
                    dst, dk = Bcm[:, tok0:tok0 + MACRO], "Bcm"
                else:
                    dst, dk = Ccm[:, tok0:tok0 + MACRO], "Ccm"
                tr.op(Sc, lambda e, dst=dst, acc=acc: e.activation(dst, acc, AF.Silu),
                      reads=("lntmp",), writes=(dk,))
            else:
                tr.op(V, lambda e, ps=ps: e.tensor_copy(dtm[0:32, :], ps[0:32, :]),
                      reads=(psk,), writes=("dtm",))
        dtr = dtm[0:32, :]
        ldt = ldtm[:]
        tr.op(V, lambda e: e.tensor_scalar(dtr, dtr, p_dtb[:], None, OP.add),
              reads=("dtm", "c_dt_bias_c"), writes=("dtm",))
        tr.op(Sc, lambda e: e.activation(chtmp[:], dtr, AF.Abs), reads=("dtm",), writes=("chtmp",))
        tr.op(Sc, lambda e: e.activation(chtmp[:], chtmp[:], AF.Exp, scale=-1.0), reads=("chtmp",), writes=("chtmp",))
        tr.op(Sc, lambda e: e.activation(chtmp[:], chtmp[:], AF.Ln, bias=1.0), reads=("chtmp",), writes=("chtmp",))
        tr.op(V, lambda e: e.scalar_tensor_tensor(dtr, dtr, 0.0, chtmp[:], OP.max, OP.add),
              reads=("dtm", "chtmp"), writes=("dtm",))
        tr.op(Sc, lambda e: e.activation(ldt, dtr, AF.Ln), reads=("dtm",), writes=("ldtm",))
        tr.op(V, lambda e: e.tensor_scalar(chtmp[:], dtr, p_A[:], None, OP.mult),
              reads=("dtm", "p_A"), writes=("chtmp",))
        if g == 0 and m == 0:
            tr.op(V, lambda e: e.memset(runt[:], 0.0), writes=("runt",))
        tr.op(V, lambda e: e.tensor_copy(runtp[:], runt[:]), reads=("runt",), writes=("runtp",))
        tr.op(V, lambda e: e.tensor_tensor_scan(acum[:], chtmp[:], chtmp[:], runt[:],
                                                OP.add, OP.bypass),
              reads=("chtmp", "runt"), writes=("acum",))
        tr.op(V, lambda e: e.tensor_copy(runt[:], acum[:, MACRO - 1:MACRO]),
              reads=("acum",), writes=("runt",))
        for lc in range(4):
            lo = lc * CHUNK
            sl = hilo[0:32, lo:lo + CHUNK]
            sll = hilo[32:64, lo:lo + CHUNK]
            if lc == 0:
                if g == 0 and m == 0:
                    tr.op(V, lambda e, lo=lo: e.tensor_copy(chtmp[:, 0:CHUNK], acum[:, lo:lo + CHUNK]),
                          reads=("acum",), writes=("chtmp",))
                else:
                    tr.op(V, lambda e, lo=lo: e.tensor_scalar(chtmp[:, 0:CHUNK], acum[:, lo:lo + CHUNK],
                                                              runtp[:], None, OP.subtract),
                          reads=("acum", "runtp"), writes=("chtmp",))
            else:
                tr.op(V, lambda e, lo=lo: e.tensor_scalar(chtmp[:, 0:CHUNK], acum[:, lo:lo + CHUNK],
                                                          acum[:, lo - 1:lo], None, OP.subtract),
                      reads=("acum",), writes=("chtmp",))
            tr.op(V, lambda e, sl=sl: e.tensor_copy(sl, chtmp[:, 0:CHUNK]), reads=("chtmp",), writes=("hilo",))
            tr.op(V, lambda e, sl=sl: e.tensor_copy(chtmp[:, CHUNK:2 * CHUNK], sl), reads=("hilo",),
                  writes=("chtmp",))
            tr.op(V, lambda e, sll=sll: e.tensor_tensor(sll, chtmp[:, 0:CHUNK], chtmp[:, CHUNK:2 * CHUNK],
                                                        OP.subtract), reads=("chtmp",), writes=("hilo",))
        tr.op("dg", lambda q: q.dma_start(out=sc_hilo[:, glob0:glob0 + MACRO], in_=hilo[:]),
              reads=("hilo",), writes=("schilo",), n=16)

    def scan_chunk(b, g, m, lc):
        glob0 = g * GROUP + m * MACRO
        cg = glob0 + lc * CHUNK
        ci = cg // CHUNK
        t0 = m * MACRO + lc * CHUNK
        rr = sc_hilo.rearrange("r (c t) -> c r t", t=CHUNK)
        tr.op("dg", lambda q: q.dma_start(out=rowsA[0:1, :].rearrange("o (h t) -> o h t", t=CHUNK),
                                          in_=rr[ci, 0:32].unsqueeze(0)),
              reads=("schilo",), writes=("rowsA",), n=16)
        tr.op("dg", lambda q: q.dma_start(out=rowsA[1:2, :].rearrange("o (h t) -> o h t", t=CHUNK),
                                          in_=rr[ci, 32:64].unsqueeze(0)),
              reads=("schilo",), writes=("rowsA",), n=16)
        if KSCAN < 11:
            return
        lo = lc * CHUNK
        tr.op("dx", lambda q: q.dma_start_transpose(out=trtmp[:, 0, :], in_=hilo[0:32, lo:lo + CHUNK]),
              reads=("hilo",), writes=("trtmp",), n=16)
        tr.op("dx", lambda q: q.dma_start_transpose(out=trtmp[:, 1, :], in_=hilo[32:64, lo:lo + CHUNK]),
              reads=("hilo",), writes=("trtmp",), n=16)
        if KSCAN < 12:
            return
        ldt = ldtm[:, lo:lo + CHUNK]
        if lc == 0:
            if cg == 0:
                tr.op(V, lambda e: e.tensor_copy(chtmp[:, 2 * CHUNK:3 * CHUNK], acum[:, lo:lo + CHUNK]),
                      reads=("acum",), writes=("chtmp",))
            else:
                tr.op(V, lambda e, lo=lo: e.tensor_scalar(chtmp[:, 2 * CHUNK:3 * CHUNK], acum[:, lo:lo + CHUNK],
                                                          runtp[:], None, OP.subtract),
                      reads=("acum", "runtp"), writes=("chtmp",))
        else:
            tr.op(V, lambda e, lo=lo: e.tensor_scalar(chtmp[:, 2 * CHUNK:3 * CHUNK], acum[:, lo:lo + CHUNK],
                                               acum[:, lo - 1:lo], None, OP.subtract),
                  reads=("acum",), writes=("chtmp",))
        tr.op(V, lambda e: e.tensor_tensor(chtmp[:, 2 * CHUNK:3 * CHUNK], ldt,
                                           chtmp[:, 2 * CHUNK:3 * CHUNK], OP.subtract),
              reads=("ldtm", "chtmp"), writes=("chtmp",))
        tr.op(V, lambda e: e.tensor_copy(chbf[:], chtmp[:, 2 * CHUNK:3 * CHUNK]),
              reads=("chtmp",), writes=("chbf",))
        tr.op(V, lambda e: e.tensor_copy(chtmp[:, 3 * CHUNK:4 * CHUNK], chbf[:]),
              reads=("chbf",), writes=("chtmp",))
        tr.op(V, lambda e: e.tensor_tensor(chbf2[:], chtmp[:, 2 * CHUNK:3 * CHUNK],
                                           chtmp[:, 3 * CHUNK:4 * CHUNK], OP.subtract),
              reads=("chtmp",), writes=("chbf2",))
        tr.op("dx", lambda q: q.dma_start_transpose(out=trtmp[:, 2, :], in_=chbf[:]),
              reads=("chbf",), writes=("trtmp",), n=16)
        tr.op("dx", lambda q: q.dma_start_transpose(out=trtmp[:, 3, :], in_=chbf2[:]),
              reads=("chbf2",), writes=("trtmp",), n=16)
        if KSCAN < 13:
            return
        tr.op(V, lambda e: e.tensor_tensor(bctm[:], trtmp[:, 2, :], trtmp[:, 3, :], OP.add),
              reads=("trtmp",), writes=("bctm",))
        tr.op(V, lambda e: e.tensor_tensor(ebc[:], trtmp[:, 0, :], trtmp[:, 1, :], OP.add),
              reads=("trtmp",), writes=("ebc",))
        tr.op(Sc, lambda e: e.activation(ebc[:], ebc[:], AF.Exp), reads=("ebc",), writes=("ebc",))
        if KSCAN < 2:
            return
        scps = pb(6, 0, 32)
        tr.op(T, lambda e: e.matmul(scps, c_ones2[:], rowsA[:, 127::128], start=True, stop=True),
              reads=("rowsA", "c_ones2"), writes=("pb6",))
        tr.op(Sc, lambda e: e.activation(lrep[:], scps, AF.Exp), reads=("pb6",), writes=("lrep",))
        tr.op(V, lambda e: e.tensor_tensor(gtmp[:, 0:32], bctm[:], scps, OP.add),
              reads=("bctm", "pb6"), writes=("gtmp",))
        tr.op(Sc, lambda e: e.activation(wtk[:], gtmp[:, 0:32], AF.Exp), reads=("gtmp",), writes=("wtk",))
        if KSCAN < 3:
            return
        tr.op(T, lambda e: e.matmul(B6B, Bcm[:, t0:t0 + CHUNK], Ccm[:, t0:t0 + CHUNK], start=True, stop=True),
              reads=("Bcm", "Ccm"), writes=("pb6",))
        tr.op(V, lambda e: e.tensor_copy(ptsb[:], B6B), reads=("pb6",), writes=("ptsb",))
        for o in range(16):
            ptx = B6C.bitcast(BF16)[:, 0:128]
            tr.op(T, lambda e, o=o: e.transpose(ptx, xhcm[:, o, lc * CHUNK:(lc + 1) * CHUNK], c_ident[:]),
                  reads=("xhcm", "c_ident"), writes=("pb6",))
            tr.op(V, lambda e, o=o: e.tensor_copy(xh_tm[:, o * 128:(o + 1) * 128], ptx),
                  reads=("pb6",), writes=("xh_tm",))
        ptx = B6C.bitcast(BF16)[:, 0:128]
        tr.op(T, lambda e: e.transpose(ptx, Bcm[:, t0:t0 + CHUNK], c_ident[:]),
              reads=("Bcm", "c_ident"), writes=("pb6",))
        tr.op(V, lambda e: e.tensor_copy(btm[:], ptx), reads=("pb6",), writes=("btm",))
        if KSCAN < 4:
            return
        if KSCAN < 5:
            return
        for gi in range(8):
            e1 = pb(2 + gi % 2)
            e1k = "pb%d" % (2 + gi % 2)
            tr.op(T, lambda e, gi=gi, e1=e1: e.matmul(e1, c_ones2[:], rowsA[:, gi * 512:(gi + 1) * 512],
                                                      start=True, stop=False),
                  reads=("rowsA", "c_ones2"), writes=(e1k,), n=0)
            tr.op(T, lambda e, e1=e1: e.matmul(e1, c_tril[:], c_trir4[:], start=False, stop=True),
                  reads=("c_tril", "c_trir4"), writes=(e1k,))
            for hh in range(4):
                h = gi * 4 + hh
                tr.op(Sc, lambda e, h=h, hh=hh, e1=e1: e.activation(
                    gtb[:, h * 128:(h + 1) * 128], e1[:, hh * 128:(hh + 1) * 128], AF.Exp,
                    bias=bctm[:, h:h + 1]), reads=(e1k, "bctm"), writes=("gtb",))
            tr.op(V, lambda e, gi=gi: e.tensor_tensor(
                gtb[:, gi * 512:(gi + 1) * 512].rearrange("p (a t) -> p a t", t=128),
                gtb[:, gi * 512:(gi + 1) * 512].rearrange("p (a t) -> p a t", t=128),
                ptsb[:].unsqueeze(1).broadcast_to((128, 4, 128)), OP.mult),
                reads=("gtb", "ptsb"), writes=("gtb",))
        if KSCAN < 6:
            return
        for yg in range(4):
            pia, pik = pb(4), "pb4"
            pie, pek = pb(5), "pb5"
            tr.op(V, lambda e, yg=yg: e.tensor_tensor(
                ytmp[:].rearrange("p (h q) -> p h q", q=64),
                xh_tm[:, yg * 512:(yg + 1) * 512].rearrange("p (h q) -> p h q", q=64),
                d_bc[:, yg * 8:(yg + 1) * 8].unsqueeze(2).broadcast_to((128, 8, 64)), OP.mult),
                reads=("xh_tm", "d_bc"), writes=("ytmp",))
            tr.op(T, lambda e, yg=yg, pia=pia: e.matmul(pia, c_ident[:], ytmp[:],
                                                        start=True, stop=False),
                  reads=("ytmp", "c_ident"), writes=(pik,), n=0)
            for hh in range(8):
                h = yg * 8 + hh
                tr.op(T, lambda e, h=h, hh=hh, pia=pia: e.matmul(
                    pia[:, hh * 64:(hh + 1) * 64], gtb[:, h * 128:(h + 1) * 128],
                    xh_tm[:, h * 64:(h + 1) * 64], start=False, stop=(hh == 7)),
                    reads=("gtb", "xh_tm"), writes=(pik,), n=1 if hh == 7 else 0)
            tr.op(T, lambda e, yg=yg, pie=pie: e.matmul(
                pie, Ccm[:, t0:t0 + CHUNK], Hbf[:, yg * 512:(yg + 1) * 512], start=True, stop=True),
                reads=("Ccm", "Hbf"), writes=(pek,))
            tr.op(V, lambda e, yg=yg, pie=pie: e.tensor_tensor(
                ytmp[:].rearrange("p (h q) -> p h q", q=64), pie.rearrange("p (h q) -> p h q", q=64),
                ebc[:, yg * 8:(yg + 1) * 8].unsqueeze(2).broadcast_to((128, 8, 64)), OP.mult),
                reads=(pek, "ebc"), writes=("ytmp",))
            tr.op(V, lambda e, yg=yg, pia=pia: e.tensor_tensor(
                y_tm[:, yg * 512:(yg + 1) * 512], pia, ytmp[:], OP.add),
                reads=(pik, "ytmp"), writes=("y_tm",))
            tr.op(V, lambda e, yg=yg: e.tensor_tensor(
                ytmp[:].rearrange("p (h q) -> p h q", q=64),
                xh_tm[:, yg * 512:(yg + 1) * 512].rearrange("p (h q) -> p h q", q=64),
                wtk[:, yg * 8:(yg + 1) * 8].unsqueeze(2).broadcast_to((128, 8, 64)), OP.mult),
                reads=("xh_tm", "wtk", "y_tm"), writes=("ytmp",))
            tr.op(T, lambda e, yg=yg, pia=pia: e.matmul(pia, btm[:], ytmp[:],
                                                        start=True, stop=True),
                  reads=("btm", "ytmp", "y_tm"), writes=(pik,))
            tr.op(V, lambda e, yg=yg: e.tensor_tensor(
                htmp[:].rearrange("p (h q) -> p h q", q=64),
                Hst[:, yg * 512:(yg + 1) * 512].rearrange("p (h q) -> p h q", q=64),
                lrep[:, yg * 8:(yg + 1) * 8].unsqueeze(2).broadcast_to((128, 8, 64)), OP.mult),
                reads=("Hst", "lrep"), writes=("htmp",))
            tr.op(V, lambda e, yg=yg, pia=pia: e.tensor_tensor(
                Hst[:, yg * 512:(yg + 1) * 512], htmp[:], pia, OP.add),
                reads=("htmp", pik), writes=("Hst",))
            tr.op(V, lambda e, yg=yg: e.tensor_copy(Hbf[:, yg * 512:(yg + 1) * 512],
                                                    Hst[:, yg * 512:(yg + 1) * 512]),
                  reads=("Hst", pek), writes=("Hbf",))
        if KSCAN < 7:
            return
        for o in range(16):
            pty = B6C.bitcast(BF16)[:, 0:128]
            tr.op(T, lambda e, o=o: e.transpose(pty, y_tm[:, o * 128:(o + 1) * 128], c_ident[:]),
                  reads=("y_tm", "c_ident"), writes=("pb6",))
            tr.op(V, lambda e, o=o: e.tensor_copy(y_cm[:, o, :], pty), reads=("pb6",), writes=("y_cm",))
        zslice = zsil[:, :, lc * CHUNK:(lc + 1) * CHUNK]
        tr.op(V, lambda e: e.tensor_tensor(zslice, y_cm[:], zslice, OP.mult),
              reads=("y_cm", "zsil"), writes=("zsil",))
        tr.op(Sc, lambda e: e.activation(y_tm[:].rearrange("p (o t) -> p o t", t=128), zslice, AF.Square),
              reads=("zsil",), writes=("y_tm",))
        for o in range(16):
            tr.op(T, lambda e, o=o: e.matmul(B6D[0:1, 0:128], c_ones128[:], y_tm[:, o * 128:(o + 1) * 128],
                                             start=(o == 0), stop=(o == 15)),
                  reads=("y_tm", "c_ones128"), writes=("pb6",), n=1 if o == 15 else 0)
        tr.op(Sc, lambda e: e.activation(sgrf[:], B6D[0:1, 0:128], AF.Ln, scale=1.0 / DIN, bias=eps_c[0:1, :]),
              reads=("pb6",), writes=("sgrf",))
        tr.op(Sc, lambda e: e.activation(sgrf[:], sgrf[:], AF.Exp, scale=-0.5), reads=("sgrf",), writes=("sgrf",))
        tr.op(V, lambda e: e.tensor_copy(sgh[:], sgrf[:]), reads=("sgrf",), writes=("sgh",))
        sgtmp = htmp[0:1, 0:256]
        tr.op(V, lambda e: e.tensor_copy(sgtmp[:, 0:128], sgh[:]), reads=("sgh",), writes=("sgtmp", "htmp"))
        tr.op(V, lambda e: e.tensor_tensor(sgtmp[:, 128:256], sgrf[:], sgtmp[:, 0:128], OP.subtract),
              reads=("sgrf", "sgtmp"), writes=("sgtmp",))
        tr.op(V, lambda e: e.tensor_copy(sgl[:], sgtmp[:, 128:256]), reads=("sgtmp",), writes=("sgl",))
        sc2 = pb(6, 0, 1)
        tr.op(T, lambda e: e.matmul(sc2, sgh[:], c_ones1r[0:1, 0:1], start=True, stop=False),
              reads=("sgh", "c_ones1r"), writes=("pb6",), n=0)
        tr.op(T, lambda e: e.matmul(sc2, sgl[:], c_ones1r[0:1, 0:1], start=False, stop=True),
              reads=("sgl", "c_ones1r"), writes=("pb6",))
        tr.op(V, lambda e, lc=lc: e.tensor_copy(scol[:, lc:lc + 1], sc2), reads=("pb6",), writes=("scol",))

    def outproj_macro(b, g, m):
        allb = tuple("pb%d" % i for i in range(8))
        for kk in range(16):
            if kk % 2 == 0:
                sl = (kk // 2) % 2
                tr.op("dw", lambda q, kk=kk, sl=sl: q.dma_start(
                    out=woring[:, sl], in_=wsc_wo[kk:kk + 2].rearrange("a p c -> p a c")),
                    writes=("wor%d" % sl,), n=16)
            for lc in range(4):
                for half in range(2):
                    ti = 2 * lc + half
                    last = (kk == 15 and half == 1)
                    pair_end = (kk % 2 == 1 and lc == 3 and half == 1)
                    tr.op(T, lambda e, kk=kk, lc=lc, half=half, ti=ti: e.matmul(
                        pst[ti][:],
                        zsil[:, kk, lc * CHUNK:(lc + 1) * CHUNK],
                        woring[:, (kk // 2) % 2, kk % 2, half * 512:(half + 1) * 512],
                        start=(kk == 0), stop=(kk == 15)),
                        reads=("zsil", "wor%d" % ((kk // 2) % 2)),
                        writes=allb if last else ("pb%d" % ti,),
                        n=1 if (last or pair_end) else 0)
        for lc in range(4):
            c = 4 * m + lc
            for half in range(2):
                ti = 2 * lc + half
                tr.op(V, lambda e, lc=lc, c=c, half=half, ti=ti: e.scalar_tensor_tensor(
                    op_b[:, c, half * 512:(half + 1) * 512], pst[ti][:], scol[:, lc:lc + 1],
                    op_b[:, c, half * 512:(half + 1) * 512], OP.mult, OP.add),
                    reads=("pb%d" % ti, "scol", "op%d" % c), writes=("op%d" % c,))

    def moe_group(b, g):
        for e in range(NE):
            for hf in range(2):
                tr.op("dw", lambda q, e=e, hf=hf: q.dma_start(
                    out=w2full[:, hf * 4:(hf + 1) * 4, :],
                    in_=wsc_m2[e, hf * 4:(hf + 1) * 4].rearrange("a p c -> p a c")),
                    writes=("w2full",), n=16)
            b2r = gtb[0:1, 0:2048].rearrange("o (a c) -> o a c", a=2)
            tr.op("dg", lambda q, e=e, b2r=b2r: q.dma_start(out=b2r[:, e % 2, :], in_=wsc_b2[e:e + 1, :]),
                  writes=("b2r%d" % (e % 2), "gtb"), n=16)
            if KMOE < 2:
                continue
            for m in range(NMAC):
                tok0 = m * MACRO
                rt_keys = tuple("rT%d_%d" % (k, cc) for k in range(8) for cc in range(4 * m, 4 * m + 4))
                for mt in range(NOC1):
                    if mt % 2 == 0:
                        mp = mt // 2
                        sl = mp % 2
                        tr.op("dw", lambda q, e=e, mp=mp, sl=sl: q.dma_start(
                            out=wring[:, sl].rearrange("p j c -> p (j c)"), in_=wsc_m1[e, mp]),
                            writes=("wring%d" % sl,), n=16)
                    sl = (mt // 2) % 2
                    ps = pb(mt % 2)
                    psk = "pb%d" % (mt % 2)
                    for k in range(8):
                        j = 2 * k + (mt % 2)
                        tr.op(T, lambda e2, j=j, k=k, ps=ps, tok0=tok0, sl=sl: e2.matmul(
                            ps, wring[:, sl, j, :], rT[:, k, tok0:tok0 + MACRO],
                            start=(k == 0), stop=(k == 7)),
                            reads=("wring%d" % sl,) + rt_keys, writes=(psk,), n=1 if k == 7 else 0)
                    if mt < 8 and KMOE >= 3:
                        bcol = p_b1[:, e * 8 + mt:e * 8 + mt + 1]
                        bcols = p_b1s[:, e * 8 + mt:e * 8 + mt + 1]
                        tr.op(V, lambda en, ps=ps, bcol=bcol: en.tensor_scalar(
                            ytmp[:], ps, bcol, 7.0, OP.add, OP.min),
                            reads=(psk, "c_b1glu_c"), writes=("ytmp",))
                        if KSWG >= 2:
                            sgm = y_cm[:, 0:4, :].rearrange("p a c -> p (a c)")
                            tr.op(Sc, lambda en, ps=ps, bcols=bcols, sgm=sgm: en.activation(
                                sgm, ps, AF.Sigmoid, scale=ALPHA, bias=bcols),
                                reads=(psk, "p_b1s", "ytmp"), writes=("sgm", "y_cm"))
                        if KSWG >= 3:
                            tr.op(V, lambda en, mt=mt, sgm=sgm: en.tensor_tensor(
                                scm[:, mt, :], ytmp[:], sgm, OP.mult),
                                reads=("ytmp", "sgm"), writes=("scm",))
                if KMOE < 4:
                    continue
                for lc in range(4):
                    c = 4 * m + lc
                    t0i = 4 + 2 * (lc % 2)
                    for half in range(2):
                        ti = t0i + half
                        for k in range(8):
                            tr.op(T, lambda e2, k=k, half=half, lc=lc, ti=ti: e2.matmul(
                                pst[ti][:],
                                scm[:, k, lc * CHUNK:(lc + 1) * CHUNK],
                                w2full[:, k, half * 512:(half + 1) * 512],
                                start=(k == 0), stop=False),
                                reads=("scm", "w2full"), writes=("pb%d" % ti,), n=0)
                        tr.op(T, lambda e2, half=half, ti=ti, e=e, b2r=b2r: e2.matmul(
                            pst[ti][:], c_ones1r[:],
                            b2r[0:1, e % 2, half * 512:(half + 1) * 512],
                            start=False, stop=True),
                            reads=("b2r%d" % (e % 2), "c_ones1r"), writes=("pb%d" % ti,))
                    if KMOE < 5:
                        continue
                    for half in range(2):
                        ti = t0i + half
                        tr.op(V, lambda en, lc=lc, c=c, half=half, ti=ti, e=e: en.scalar_tensor_tensor(
                            op_b[:, c, half * 512:(half + 1) * 512], pst[ti][:], route8[:, c, e:e + 1],
                            op_b[:, c, half * 512:(half + 1) * 512], OP.mult, OP.add),
                            reads=("pb%d" % ti, "route%d" % c, "op%d" % c), writes=("op%d" % c,))

    with nc.Fori(0, B_PER_CORE) as b:
        tr.op(V, lambda e: e.memset(Hst[:], 0.0), writes=("Hst",))
        tr.op(V, lambda e: e.memset(Hbf[:], 0.0), writes=("Hbf",))
        for o in range(18):
            tr.op(V, lambda e, o=o: e.memset(halo[:, o, :], 0.0), writes=("halo%d" % o,))
        for g in range(NG):
            phase_a(b, g, 0, 4)
            for m in range(NMAC):
                if STAGE >= 2:
                    in_proj_macro(b, g, m)
                if m == 0:
                    phase_a(b, g, 4, NCHK)
                if STAGE >= 3:
                    for lc in range(4):
                        scan_chunk(b, g, m, lc)
                if STAGE >= 4:
                    outproj_macro(b, g, m)
            if STAGE >= 5:
                moe_group(b, g)
            tr.op("dg", lambda q, g=g: q.dma_start(
                out=out[b, g * GROUP:(g + 1) * GROUP, :].rearrange("(c p) d -> p c d", p=128),
                in_=op_b[:]),
                reads=tuple("op%d" % c for c in range(NCHK)), writes=(), n=16)
        tr.join_all()
        tr.reset_sems(b)
    nc.sync.wait_ge(tr.barsem, 10 + 10 * B_PER_CORE)
    return nc


def kernel(**inputs):
    h = host_layouts(inputs)
    x_full = np.ascontiguousarray(np.asarray(inputs["L0_attn_x"], np.float32))
    nc = build()
    in_maps = []
    for core in range(N_CORES):
        m = {"x": x_full[core * B_PER_CORE:(core + 1) * B_PER_CORE]}
        for name, shape, dt in INPUT_SPECS:
            m[name] = h[name]
        in_maps.append(m)
    res = run_bass_kernel_spmd(nc, in_maps, core_ids=list(range(N_CORES)))
    outs = [res.results[i]["out"] for i in range(N_CORES)]
    return np.concatenate(outs, axis=0)


if __name__ == "__main__":
    nc = build()
    n = sum(1 for _ in nc.all_instructions())
    print("instructions:", n)
